# version 29
# baseline (speedup 1.0000x reference)
import contextlib
import numpy as np

import concourse.bass as bass
import concourse.mybir as mybir
from concourse.bass_utils import run_bass_kernel_spmd

F32 = mybir.dt.float32
BF16 = mybir.dt.bfloat16
FP16 = mybir.dt.float16
I32 = mybir.dt.int32
AF = mybir.ActivationFunctionType
ALU = mybir.AluOpType
AX = mybir.AxisListType

D = 1024
NST = 256
F1 = 2736
F2 = 4096
NCHUNK = D // 128
F2CHUNK = F2 // 128
EPS_LN = 1e-5
EPS_S = 1e-6
EPS_BL = 1e-5

FSL = [(i * 128, 128) for i in range(21)] + [(2688, 48)]

ENGS = ("pe", "dve", "act", "pool", "sp")


class Op:
    __slots__ = ("eng", "waits", "emit", "sem", "inc")

    def __init__(self, eng, waits, emit, sem, inc):
        self.eng, self.waits, self.emit, self.sem, self.inc = eng, waits, emit, sem, inc


class Ctx:

    def __init__(self, relax=True):
        self.relax = relax
        self.q = {e: [] for e in ENGS}
        self.cnt = {s: 0 for s in ENGS}
        self.waited = {e: {} for e in ENGS}
        self.writer = {}
        self.readers = {}

    def op(self, eng, r=(), w=(), f=None, dma=None):
        if dma is not None:
            sem = "D:" + dma
            inc = 16
            self.cnt.setdefault(sem, 0)
        else:
            sem = eng
            inc = 1
        need = {}
        if dma is not None and self.cnt[sem] > 0:
            need[sem] = self.cnt[sem]
        for k in r:
            st = self.writer.get(k)
            if st is not None:
                need[st[0]] = max(need.get(st[0], 0), st[1])
        for k in w:
            st = self.writer.get(k)
            if st is not None:
                need[st[0]] = max(need.get(st[0], 0), st[1])
            for s, v in self.readers.get(k, {}).items():
                need[s] = max(need.get(s, 0), v)
        waits = []
        wd = self.waited[eng]
        for s, v in need.items():
            if wd.get(s, 0) < v:
                waits.append((s, v))
                wd[s] = v
        self.cnt[sem] += inc
        stamp = (sem, self.cnt[sem])
        for k in w:
            self.writer[k] = stamp
            self.readers[k] = {}
        for k in r:
            d = self.readers.setdefault(k, {})
            d[stamp[0]] = max(d.get(stamp[0], 0), stamp[1])
        self.q[eng].append(Op(eng, waits, f, sem, inc))

    def barrier(self):
        tgt = dict(self.cnt)
        for e in ENGS:
            waits = []
            wd = self.waited[e]
            for s, v in tgt.items():
                if v > 0 and wd.get(s, 0) < v:
                    waits.append((s, v))
                    wd[s] = v
            if waits:
                self.q[e].append(Op(e, waits, None, None, 0))
        self.writer = {}
        self.readers = {}


def bcast_ap(t, chunks, tt):
    return bass.AP(t, 0, [[tt, 128], [0, chunks], [1, tt]])


def row01(t, n, off=0, rowlen=1):
    return bass.AP(t, off, [[rowlen, 1], [0, n]])


def build(T=4096, TT=256, relax=True):
    NT = T // TT
    nc = bass.Bass("TRN2", target_bir_lowering=False)

    dram = lambda name, shape, dt, kind: nc.dram_tensor(name, shape, dt, kind=kind).ap()
    xT = dram("xT", [D, T], F32, "ExternalInput")
    ll = dram("ll", [128, 2], F32, "ExternalInput")
    bwT = dram("bwT", [D, NST], F32, "ExternalInput")
    cwT = dram("cwT", [NST, D], F32, "ExternalInput")
    w1T = dram("w1T", [D, F1], F32, "ExternalInput")
    w3T = dram("w3T", [D, F1], F32, "ExternalInput")
    w2T = dram("w2T", [22 * 128, D], F32, "ExternalInput")
    dsk = dram("dsk", [128, NCHUNK], F32, "ExternalInput")
    snw = dram("snw", [128, NCHUNK], F32, "ExternalInput")
    fnw = dram("fnw", [128, NCHUNK], F32, "ExternalInput")
    identd = dram("ident", [128, 128], F32, "ExternalInput")
    f1Td = dram("f1T", [D, F2], F32, "ExternalInput")
    b1d = dram("b1", [128, F2CHUNK], F32, "ExternalInput")
    f2Td = dram("f2T", [F2, D], F32, "ExternalInput")
    b2d = dram("b2", [128, NCHUNK], F32, "ExternalInput")
    outT = dram("outT", [D, T], F32, "ExternalOutput")
    xqh = dram("xqh", [D, T], BF16, "Internal")
    xoh = dram("xoh", [D, T], F32, "Internal")
    iah = dram("iah", [1, T], F32, "Internal")
    thrh = nc.dram_tensor("thrh", [1, 8], F32, kind="Internal")
    thrh_ap = thrh.ap()

    rearr = lambda ap: ap.rearrange("(c p) t -> p c t", p=128)

    es = contextlib.ExitStack()
    ctx = Ctx(relax=relax)
    sbA = lambda st, name, shape, dt: st.enter_context(nc.sbuf_tensor(name, shape, dt))
    psA = lambda st, name, shape: st.enter_context(nc.psum_tensor(name, shape, F32))

    IDF = sbA(es, "IDF", [128, 128], F32)
    ONESR = sbA(es, "ONESR", [1, 128], F32)
    ONEB = sbA(es, "ONEB", [128, 1], BF16)

    ctx.op("sp", w=["IDF"], dma="vl",
           f=lambda e: e.dma_start(out=IDF[:], in_=identd[:]))
    ctx.op("dve", w=["ONESR"], f=lambda e: e.memset(ONESR[:], 1.0))
    ctx.op("dve", w=["onesbf"], f=lambda e: e.memset(ONEB[:], 1.0))

    def bcast(row_ap, row_keys, dst, kdst, pgrp, kpsum):
        ctx.op("pe", r=list(row_keys) + ["ONESR"], w=[kpsum],
               f=lambda e: e.matmul(pgrp[:, 0:TT], ONESR[:], row_ap,
                                    start=True, stop=True))
        ctx.op("act", r=[kpsum], w=[kdst],
               f=lambda e: e.activation(out=dst[:], in_=pgrp[:, 0:TT], func=AF.Copy))

    def round_chain(src_ap, src_keys, iq, kiq, dbuf, kd, mask_ap, kmask, emit_out):
        ctx.op("dve", r=list(src_keys), w=[kiq],
               f=lambda e: e.tensor_copy(out=iq, in_=src_ap))
        ctx.op("dve", r=list(src_keys) + [kiq], w=[kd],
               f=lambda e: e.scalar_tensor_tensor(out=dbuf, in0=iq, scalar=-1.0,
                                                  in1=src_ap, op0=ALU.mult, op1=ALU.add))
        ctx.op("dve", r=[kd], w=[kd],
               f=lambda e: e.tensor_scalar(out=dbuf, in0=dbuf, scalar1=8.0,
                                           scalar2=None, op0=ALU.add))
        for s in (8.0, 4.0, 2.0, 1.0):
            ctx.op("dve", r=[kd], w=[kmask],
                   f=lambda e, s=s: e.tensor_scalar(out=mask_ap, in0=dbuf,
                                                    scalar1=s - 0.5, scalar2=None,
                                                    op0=ALU.is_gt))
            ctx.op("dve", r=[kd, kmask], w=[kd],
                   f=lambda e, s=s: e.scalar_tensor_tensor(out=dbuf, in0=mask_ap,
                                                           scalar=-s, in1=dbuf,
                                                           op0=ALU.mult, op1=ALU.add))
        emit_out()

    s1 = contextlib.ExitStack()
    bwsb = sbA(s1, "bwsb", [128, NCHUNK, NST], BF16)
    cwsb = sbA(s1, "cwsb", [128, 2, D], BF16)
    w1sb = sbA(s1, "w1sb", [128, NCHUNK, F1], BF16)
    w3sb = sbA(s1, "w3sb", [128, NCHUNK, F1], BF16)
    w2sb = sbA(s1, "w2sb", [128, 22, D], BF16)
    X0 = sbA(s1, "X0", [128, NCHUNK, TT], F32)
    X1 = sbA(s1, "X1", [128, NCHUNK, TT], F32)
    Xbf = sbA(s1, "Xbf", [128, NCHUNK, TT], BF16)
    S1 = sbA(s1, "S1b", [128, NCHUNK, TT], BF16)
    BU = sbA(s1, "BU", [128, 2, TT], F32)
    HSB = sbA(s1, "HSB", [128, 2, TT], BF16)
    STATE = sbA(s1, "STATE", [128, 2, 1], F32)
    XB2 = sbA(s1, "XB2", [128, NCHUNK, TT], F32)
    XOUT = sbA(s1, "XOUT", [128, NCHUNK, TT], F32)
    GV = sbA(s1, "GV", [128, 22, TT], BF16)
    GS0 = sbA(s1, "GS0", [128, TT], BF16)
    GS1 = sbA(s1, "GS1", [128, TT], BF16)
    SCA = sbA(s1, "SCA", [128, TT], F32)
    SCT = sbA(s1, "SCT", [128, TT], F32)
    XI1 = sbA(s1, "XI1", [128, TT], I32)
    DHP = sbA(s1, "DHP", [128, TT], FP16)
    MHP = sbA(s1, "MHP", [128, TT], FP16)
    GBH = sbA(s1, "GBH", [128, TT], FP16)
    XQS = sbA(s1, "XQS", [128, TT], BF16)
    COLS = sbA(s1, "COLS", [128, 2], F32)
    BC = [sbA(s1, "BC%d" % i, [128, TT], F32) for i in range(3)]
    ROWS = sbA(s1, "ROWS", [1, 6, TT], F32)
    C1N = sbA(s1, "C1N", [128, 2, 1], F32)
    DP = sbA(s1, "DP", [128, NCHUNK], F32)
    SNW = sbA(s1, "SNW", [128, NCHUNK], F32)
    FNW = sbA(s1, "FNW", [128, NCHUNK], F32)
    LLS = sbA(s1, "LLS", [128, 2], F32)

    psRow = psA(s1, "psRow", [1, 2, TT])
    PaA = psA(s1, "PaA", [128, TT])
    PaB = psA(s1, "PaB", [128, TT])
    PaC = psA(s1, "PaC", [128, TT])
    psF = psA(s1, "psF", [128, NCHUNK, TT])
    kF = lambda dc: "pb:F%d" % (dc // 2)

    ctx.op("sp", w=["LLS"], dma="vl",
           f=lambda e: e.dma_start(out=LLS[:], in_=ll[:]))
    ctx.op("sp", w=["DPr"], dma="vl",
           f=lambda e: e.dma_start(out=DP[:], in_=dsk[:]))
    ctx.op("sp", w=["SNW"], dma="vl",
           f=lambda e: e.dma_start(out=SNW[:], in_=snw[:]))
    ctx.op("sp", w=["FNW"], dma="vl",
           f=lambda e: e.dma_start(out=FNW[:], in_=fnw[:]))
    ctx.op("sp", w=["X0"], dma="vl",
           f=lambda e: e.dma_start(out=X0[:], in_=rearr(bwT)))
    ctx.op("pool", w=["cw"], dma="wc",
           f=lambda e: e.dma_start(out=cwsb[:], in_=cwT.rearrange("(c p) t -> p c t", p=128)))
    ctx.op("pool", w=["w1"], dma="wc",
           f=lambda e: e.dma_start(out=w1sb[:], in_=rearr(w1T)))
    ctx.op("pool", w=["w3"], dma="wc",
           f=lambda e: e.dma_start(out=w3sb[:], in_=rearr(w3T)))
    ctx.op("pool", w=["w2"], dma="wc",
           f=lambda e: e.dma_start(out=w2sb[:], in_=w2T.rearrange("(c p) t -> p c t", p=128)))

    for c in range(NCHUNK):
        ctx.op("act", r=["X0", "SNW"], w=["bw"],
               f=lambda e, c=c: e.activation(out=bwsb[:, c, :], in_=X0[:, c, :],
                                             func=AF.Copy, scale=SNW[:, c:c + 1]))
    for n in range(2):
        def emit_c1(e, n=n):
            for c in range(NCHUNK):
                mm = e.matmul(psF[:, n, 0:1], bwsb[:, c, n * 128:(n + 1) * 128], ONEB[:],
                              start=(c == 0), stop=(c == NCHUNK - 1))
            return mm
        ctx.op("pe", r=["bw", "onesbf"], w=[kF(n)], f=emit_c1)
    for n in range(2):
        ctx.op("dve", r=[kF(n)], w=["C1N"],
               f=lambda e, n=n: e.tensor_scalar(out=C1N[:, n, :], in0=psF[:, n, 0:1],
                                                scalar1=-1.0, scalar2=None, op0=ALU.mult))
    ctx.op("dve", r=["DPr", "SNW"], w=["DP"],
           f=lambda e: e.tensor_tensor(out=DP[:], in0=DP[:], in1=SNW[:], op=ALU.mult))
    ctx.op("act", r=["LLS"], w=["LLS"],
           f=lambda e: e.activation(out=LLS[:], in_=LLS[:], func=AF.Sigmoid))
    ctx.op("dve", w=["STATE"], f=lambda e: e.memset(STATE[:], 0.0))

    def emit_sum(e, src_, row):
        for c in range(NCHUNK):
            mm = e.matmul(psRow[:, row, :], ONEB[:], src_[:, c, :],
                          start=(c == 0), stop=(c == NCHUNK - 1))
        return mm

    def p1_A(it):
        t0 = it * TT
        tsl = slice(t0, t0 + TT)
        X = X0 if it % 2 == 0 else X1
        kX = "X%d" % (it % 2)
        ctx.op("sp", w=[kX], dma="xl",
               f=lambda e, X=X, tsl=tsl: e.dma_start(out=X[:], in_=rearr(xT)[:, :, tsl]))
        ctx.op("dve", r=[kX], w=["Xbf"],
               f=lambda e, X=X: e.tensor_copy(out=Xbf[:], in_=X[:]))
        ctx.op("act", r=[kX], w=["S1"],
               f=lambda e, X=X: e.activation(out=S1[:], in_=X[:], func=AF.Square))
        ctx.op("pe", r=["Xbf", "onesbf"], w=["pb:row"], f=lambda e: emit_sum(e, Xbf, 0))
        ctx.op("pe", r=["S1", "onesbf"], w=["pb:row"],
               f=lambda e: emit_sum(e, S1, 1))
        for n in range(2):
            def emit_bu(e, n=n):
                for c in range(NCHUNK):
                    mm = e.matmul((PaA if n == 0 else PaB)[:],
                                  bwsb[:, c, n * 128:(n + 1) * 128], Xbf[:, c, :],
                                  start=(c == 0), stop=(c == NCHUNK - 1))
                return mm
            ctx.op("pe", r=["bw", "Xbf"], w=["pb:" + ("A" if n == 0 else "B")], f=emit_bu)
        ctx.op("dve", r=["pb:row"], w=["rows"],
               f=lambda e: e.tensor_scalar(out=ROWS[:, 0, :], in0=psRow[:, 0, :],
                                           scalar1=1.0 / D, scalar2=None, op0=ALU.mult))
        ctx.op("dve", r=["pb:row"], w=["rows"],
               f=lambda e: e.tensor_scalar(out=ROWS[:, 1, :], in0=psRow[:, 1, :],
                                           scalar1=1.0 / D, scalar2=None, op0=ALU.mult))
        ctx.op("dve", r=["rows"], w=["rows"],
               f=lambda e: e.tensor_tensor(out=ROWS[:, 2, :], in0=ROWS[:, 0, :],
                                           in1=ROWS[:, 0, :], op=ALU.mult))
        ctx.op("dve", r=["rows"], w=["rows"],
               f=lambda e: e.tensor_tensor(out=ROWS[:, 2, :], in0=ROWS[:, 1, :],
                                           in1=ROWS[:, 2, :], op=ALU.subtract))
        ctx.op("dve", r=["rows"], w=["rows"],
               f=lambda e: e.tensor_scalar(out=ROWS[:, 3, :], in0=ROWS[:, 2, :],
                                           scalar1=EPS_LN, scalar2=None, op0=ALU.add))
        ctx.op("dve", r=["rows"], w=["rows"],
               f=lambda e: e.tensor_scalar(out=ROWS[:, 4, :], in0=ROWS[:, 2, :],
                                           scalar1=1.0 + EPS_S, scalar2=EPS_S * EPS_LN,
                                           op0=ALU.mult, op1=ALU.add))
        ctx.op("dve", r=["rows"], w=["rows"],
               f=lambda e: e.reciprocal(out=ROWS[:, 1:3, :], in_=ROWS[:, 3:5, :]))
        ctx.op("act", r=["rows"], w=["rows"],
               f=lambda e: e.activation(out=ROWS[:, 3:5, :], in_=ROWS[:, 1:3, :], func=AF.Sqrt))
        ctx.op("dve", r=["rows"], w=["rows"],
               f=lambda e: e.tensor_tensor(out=ROWS[:, 1, :], in0=ROWS[:, 4, :],
                                           in1=ROWS[:, 0, :], op=ALU.mult))
        for bi, ri in ((0, 4), (1, 1), (2, 3)):
            bcast(ROWS[:, ri, :], ["rows"], BC[bi], "BC%d" % bi, PaC, "pb:C")
        for n in range(2):
            kP = "pb:" + ("A" if n == 0 else "B")
            ctx.op("dve", r=[kP, "BC0"], w=["BU"],
                   f=lambda e, n=n: e.tensor_tensor(out=BU[:, n, :],
                                                    in0=(PaA if n == 0 else PaB)[:],
                                                    in1=BC[0][:], op=ALU.mult))
            ctx.op("dve", r=["BC1", "C1N", "BU"], w=["BU"],
                   f=lambda e, n=n: e.scalar_tensor_tensor(out=BU[:, n, :], in0=BC[1][:],
                                                           scalar=C1N[:, n, :], in1=BU[:, n, :],
                                                           op0=ALU.mult, op1=ALU.add))
            ctx.op("dve", r=["BU", "LLS", "STATE"], w=["BU"],
                   f=lambda e, n=n: e.tensor_tensor_scan(
                       out=BU[:, n, :], data0=bass.AP(LLS, n, [[2, 128], [0, TT]]),
                       data1=BU[:, n, :], initial=STATE[:, n, :],
                       op0=ALU.mult, op1=ALU.add))
            ctx.op("dve", r=["BU"], w=["STATE"],
                   f=lambda e, n=n: e.tensor_copy(out=STATE[:, n, :], in_=BU[:, n, TT - 1:TT]))
        bcast(ROWS[:, 0, :], ["rows"], BC[1], "BC1", PaC, "pb:C")
        ctx.op("dve", r=["BU"], w=["hs"],
               f=lambda e: e.tensor_copy(out=HSB[:], in_=BU[:]))

    def p1_B(it):
        X = X0 if it % 2 == 0 else X1
        kX = "X%d" % (it % 2)
        for dc in range(NCHUNK):
            kY = "pb:" + ("A" if dc % 2 == 0 else "B")
            PY = PaA if dc % 2 == 0 else PaB
            def emit_y(e, dc=dc, PY=PY):
                for n in range(2):
                    mm = e.matmul(PY[:], cwsb[:, n, dc * 128:(dc + 1) * 128], HSB[:, n, :],
                                  start=(n == 0), stop=(n == 1))
                return mm
            ctx.op("pe", r=["cw", "hs"], w=[kY], f=emit_y)
            ctx.op("dve", r=[kX, "BC1"], w=["SCT"],
                   f=lambda e, dc=dc, X=X: e.tensor_tensor(out=SCT[:], in0=X[:, dc, :],
                                                           in1=BC[1][:], op=ALU.subtract))
            ctx.op("dve", r=["BC0", "BC2", "DP"], w=["SCA"],
                   f=lambda e, dc=dc: e.scalar_tensor_tensor(out=SCA[:], in0=BC[0][:],
                                                             scalar=DP[:, dc:dc + 1],
                                                             in1=BC[2][:],
                                                             op0=ALU.mult, op1=ALU.add))
            ctx.op("dve", r=["SCT", "SCA"], w=["SCA"],
                   f=lambda e: e.tensor_tensor(out=SCA[:], in0=SCT[:], in1=SCA[:], op=ALU.mult))
            ctx.op("dve", r=["SCA", kY], w=["XB2"],
                   f=lambda e, dc=dc, PY=PY: e.tensor_tensor(out=XB2[:, dc, :], in0=SCA[:],
                                                             in1=PY[:], op=ALU.add))
        ctx.op("act", r=["XB2"], w=["S1"],
               f=lambda e: e.activation(out=S1[:], in_=XB2[:], func=AF.Square))
        ctx.op("pe", r=["S1", "onesbf"], w=["pb:row"], f=lambda e: emit_sum(e, S1, 0))
        ctx.op("dve", r=["pb:row"], w=["rows"],
               f=lambda e: e.tensor_scalar(out=ROWS[:, 5, :], in0=psRow[:, 0, :],
                                           scalar1=1.0 / D, scalar2=EPS_S,
                                           op0=ALU.mult, op1=ALU.add))
        ctx.op("dve", r=["rows"], w=["rows"],
               f=lambda e: e.reciprocal(out=ROWS[:, 2, :], in_=ROWS[:, 5, :]))
        ctx.op("act", r=["rows"], w=["rows"],
               f=lambda e: e.activation(out=ROWS[:, 5, :], in_=ROWS[:, 2, :], func=AF.Sqrt))
        bcast(ROWS[:, 5, :], ["rows"], BC[0], "BC0", PaC, "pb:C")
        for dc in range(NCHUNK):
            ctx.op("dve", r=["XB2", "FNW", "BC0"], w=["S1"],
                   f=lambda e, dc=dc: e.scalar_tensor_tensor(out=S1[:, dc, :],
                                                             in0=XB2[:, dc, :],
                                                             scalar=FNW[:, dc:dc + 1],
                                                             in1=BC[0][:],
                                                             op0=ALU.mult, op1=ALU.mult))

    def p1_CD(it, pend=None):
        pend = list(pend or [])

        def drain(n):
            for _ in range(min(n, len(pend))):
                pend.pop(0)()
        pa_slots = [(PaA, "pb:A"), (PaB, "pb:B"), (PaC, "pb:C")]
        for fs in range(22):
            off, fsz = FSL[fs]
            pg, kg = pa_slots[(2 * fs) % 3]
            pv, kv = pa_slots[(2 * fs + 1) % 3]
            gs = GS0 if fs % 2 == 0 else GS1
            kgs = "GS%d" % (fs % 2)
            def emit_g(e, off=off, fsz=fsz, pg=pg):
                for c in range(NCHUNK):
                    mm = e.matmul(pg[0:fsz, :], w1sb[:, c, off:off + fsz], S1[:, c, :],
                                  start=(c == 0), stop=(c == NCHUNK - 1))
                return mm
            ctx.op("pe", r=["w1", "S1"], w=[kg], f=emit_g)
            def emit_v(e, off=off, fsz=fsz, pv=pv):
                for c in range(NCHUNK):
                    mm = e.matmul(pv[0:fsz, :], w3sb[:, c, off:off + fsz], S1[:, c, :],
                                  start=(c == 0), stop=(c == NCHUNK - 1))
                return mm
            ctx.op("pe", r=["w3", "S1"], w=[kv], f=emit_v)
            ctx.op("act", r=[kg], w=[kgs],
                   f=lambda e, fsz=fsz, pg=pg, gs=gs: e.activation(
                       out=gs[0:fsz, :], in_=pg[0:fsz, :], func=AF.Silu))
            ctx.op("dve", r=[kgs, kv], w=["GV"],
                   f=lambda e, fs=fs, fsz=fsz, pv=pv, gs=gs: e.tensor_tensor(
                       out=GV[0:fsz, fs, :], in0=gs[0:fsz, :], in1=pv[0:fsz, :],
                       op=ALU.mult))
            drain(3)
        for dc in range(NCHUNK):
            def emit_f(e, dc=dc):
                for fs in range(22):
                    off, fsz = FSL[fs]
                    mm = e.matmul(psF[:, dc, :], w2sb[0:fsz, fs, dc * 128:(dc + 1) * 128],
                                  GV[0:fsz, fs, :], start=(fs == 0), stop=(fs == 21))
                return mm
            ctx.op("pe", r=["w2", "GV"], w=[kF(dc)], f=emit_f)
            drain(5)
        drain(len(pend))

    def p1_E(it):
        t0 = it * TT
        tsl = slice(t0, t0 + TT)
        X = X0 if it % 2 == 0 else X1
        kX = "X%d" % (it % 2)
        ctx.op("dve", r=[kX, "XB2"], w=[kX],
               f=lambda e, X=X: e.tensor_tensor(out=X[:], in0=X[:], in1=XB2[:], op=ALU.add))
        for dc in range(NCHUNK):
            ctx.op("dve", r=[kX, kF(dc)], w=["XOUT"],
                   f=lambda e, dc=dc, X=X: e.tensor_tensor(out=XOUT[:, dc, :],
                                                           in0=X[:, dc, :],
                                                           in1=psF[:, dc, :], op=ALU.add))
        ctx.op("pool", r=["XOUT"], w=["xoh"], dma="xo",
               f=lambda e, tsl=tsl: e.dma_start(out=rearr(xoh)[:, :, tsl], in_=XOUT[:]))
        ctx.op("dve", r=["XOUT", "GV"], w=["GV"],
               f=lambda e: e.tensor_copy(out=GV[:, 0:8, :], in_=XOUT[:]))
        ctx.op("act", r=["XOUT", "GV"], w=["GV"],
               f=lambda e: e.activation(out=GV[:, 8:16, :], in_=XOUT[:], func=AF.Square))
        ctx.op("pe", r=["GV", "onesbf"], w=["pb:row"],
               f=lambda e: emit_sum(e, GV[:, 0:8, :], 0))
        ctx.op("pe", r=["GV", "onesbf"], w=["pb:row"],
               f=lambda e: emit_sum(e, GV[:, 8:16, :], 1))
        ctx.op("dve", r=["pb:row"], w=["rows"],
               f=lambda e: e.tensor_scalar(out=ROWS[:, 0, :], in0=psRow[:, 0, :],
                                           scalar1=1.0 / D, scalar2=None, op0=ALU.mult))
        ctx.op("dve", r=["pb:row"], w=["rows"],
               f=lambda e: e.tensor_scalar(out=ROWS[:, 1, :], in0=psRow[:, 1, :],
                                           scalar1=1.0 / D, scalar2=None, op0=ALU.mult))
        ctx.op("dve", r=["rows"], w=["rows"],
               f=lambda e: e.tensor_tensor(out=ROWS[:, 2, :], in0=ROWS[:, 0, :],
                                           in1=ROWS[:, 0, :], op=ALU.mult))
        ctx.op("dve", r=["rows"], w=["rows"],
               f=lambda e: e.tensor_tensor(out=ROWS[:, 2, :], in0=ROWS[:, 1, :],
                                           in1=ROWS[:, 2, :], op=ALU.subtract))
        ctx.op("dve", r=["rows"], w=["rows"],
               f=lambda e: e.tensor_scalar(out=ROWS[:, 3, :], in0=ROWS[:, 2, :],
                                           scalar1=1.0 + EPS_BL, scalar2=EPS_BL * EPS_LN,
                                           op0=ALU.mult, op1=ALU.add))
        ctx.op("dve", r=["rows"], w=["rows"],
               f=lambda e: e.reciprocal(out=ROWS[:, 4, :], in_=ROWS[:, 3, :]))
        ctx.op("act", r=["rows"], w=["rows"],
               f=lambda e: e.activation(out=ROWS[:, 3, :], in_=ROWS[:, 4, :], func=AF.Sqrt))
        bcast(ROWS[:, 0, :], ["rows"], BU[:, 0, :], "BU", PaC, "pb:C")
        ctx.op("dve", r=["XOUT", "BU"], w=["XB2"],
               f=lambda e: e.tensor_tensor(out=XB2[:], in0=XOUT[:],
                                           in1=bass.AP(BU, 0, [[2 * TT, 128],
                                                               [0, NCHUNK], [1, TT]]),
                                           op=ALU.subtract))
        ctx.op("dve", r=["XB2"], w=["SCA"],
               f=lambda e: e.tensor_tensor(out=SCA[:], in0=XB2[:, 0, :], in1=XB2[:, 1, :],
                                           op=ALU.max))
        ctx.op("dve", r=["XB2"], w=["SCT"],
               f=lambda e: e.tensor_tensor(out=SCT[:], in0=XB2[:, 0, :], in1=XB2[:, 1, :],
                                           op=ALU.min))
        for c in range(2, NCHUNK):
            ctx.op("dve", r=["XB2", "SCA"], w=["SCA"],
                   f=lambda e, c=c: e.tensor_tensor(out=SCA[:], in0=SCA[:], in1=XB2[:, c, :],
                                                    op=ALU.max))
            ctx.op("dve", r=["XB2", "SCT"], w=["SCT"],
                   f=lambda e, c=c: e.tensor_tensor(out=SCT[:], in0=SCT[:], in1=XB2[:, c, :],
                                                    op=ALU.min))
        ctx.op("dve", r=["SCT", "SCA"], w=["SCA"],
               f=lambda e: e.scalar_tensor_tensor(out=SCA[:], in0=SCT[:], scalar=-1.0,
                                                  in1=SCA[:], op0=ALU.mult, op1=ALU.max))
        for k in range(TT // 128):
            pT = PaC
            kT = "pb:C"
            ctx.op("pe", r=["SCA", "IDF"], w=[kT],
                   f=lambda e, k=k, pT=pT: e.transpose(pT[:, 0:128],
                                                       SCA[:, k * 128:(k + 1) * 128],
                                                       IDF[:]))
            ctx.op("dve", r=[kT], w=["COLS"],
                   f=lambda e, k=k, pT=pT: e.tensor_reduce(out=COLS[:, k:k + 1],
                                                           in_=pT[:, 0:128], axis=AX.X,
                                                           op=ALU.max))
            ctx.op("pe", r=["COLS", "IDF"], w=["pb:row"],
                   f=lambda e, k=k: e.transpose(psRow[:, 1, k * 128:(k + 1) * 128],
                                                COLS[:, k:k + 1], IDF[:]))
        ctx.op("dve", r=["rows", "pb:row"], w=["rows"],
               f=lambda e: e.tensor_tensor(out=ROWS[:, 4, :], in0=ROWS[:, 3, :],
                                           in1=psRow[:, 1, :], op=ALU.mult))
        ctx.op("dve", r=["rows"], w=["rows"],
               f=lambda e: e.tensor_scalar(out=ROWS[:, 4, :], in0=ROWS[:, 4, :],
                                           scalar1=1e-5, scalar2=None, op0=ALU.max))
        ctx.op("dve", r=["rows"], w=["rows"],
               f=lambda e: e.reciprocal(out=ROWS[:, 5, :], in_=ROWS[:, 4, :]))
        ctx.op("dve", r=["rows"], w=["rows"],
               f=lambda e: e.tensor_tensor(out=ROWS[:, 5, :], in0=ROWS[:, 3, :],
                                           in1=ROWS[:, 5, :], op=ALU.mult))
        ctx.op("dve", r=["rows"], w=["rows"],
               f=lambda e: e.tensor_scalar(out=ROWS[:, 5, :], in0=ROWS[:, 5, :],
                                           scalar1=127.0, scalar2=None, op0=ALU.mult))
        ctx.op("dve", r=["rows"], w=["rows"],
               f=lambda e: e.tensor_scalar(out=ROWS[:, 4, :], in0=ROWS[:, 4, :],
                                           scalar1=1.0 / 127.0, scalar2=None, op0=ALU.mult))
        bcast(ROWS[:, 5, :], ["rows"], GBH, "GBH", PaC, "pb:C")
        ctx.op("sp", r=["rows"], w=["iah"], dma="ia",
               f=lambda e, tsl=tsl: e.dma_start(out=iah[0:1, tsl], in_=ROWS[:, 4, :]))

    def p1_y1s(it):
        X = X0 if it % 2 == 0 else X1
        kX = "X%d" % (it % 2)
        for c in range(NCHUNK):
            ctx.op("dve", r=["XB2", "GBH"], w=[kX],
                   f=lambda e, c=c, X=X: e.tensor_tensor(out=X[:, c, :], in0=XB2[:, c, :],
                                                         in1=GBH[:], op=ALU.mult))

    def p1_F(it):
        t0 = it * TT
        X = X0 if it % 2 == 0 else X1
        kX = "X%d" % (it % 2)
        th = []
        for c in range(NCHUNK):
            th.append(lambda c=c, X=X: ctx.op("act", r=[kX], w=["XI1"],
                   f=lambda e, c=c, X=X: e.activation(out=XI1[:], in_=X[:, c, :],
                                                      func=AF.Copy)))
            th.append(lambda: ctx.op("act", r=["XI1"], w=["DHP"],
                   f=lambda e: e.activation(out=DHP[:], in_=XI1[:], func=AF.Copy,
                                            bias=-8.0)))
            th.append(lambda c=c, X=X: ctx.op("dve", r=[kX, "DHP"], w=["DHP"],
                   f=lambda e, c=c, X=X: e.tensor_tensor(out=DHP[:], in0=X[:, c, :],
                                                         in1=DHP[:], op=ALU.subtract)))
            for s in (8.0, 4.0, 2.0, 1.0):
                th.append(lambda s=s: ctx.op("dve", r=["DHP"], w=["MHP"],
                       f=lambda e, s=s: e.tensor_scalar(out=MHP[:], in0=DHP[:],
                                                        scalar1=s - 0.5, scalar2=None,
                                                        op0=ALU.is_gt)))
                th.append(lambda s=s: ctx.op("dve", r=["DHP", "MHP"], w=["DHP"],
                       f=lambda e, s=s: e.scalar_tensor_tensor(out=DHP[:], in0=MHP[:],
                                                               scalar=-s, in1=DHP[:],
                                                               op0=ALU.mult, op1=ALU.add)))
            th.append(lambda c=c, X=X: ctx.op("dve", r=[kX, "DHP"], w=["XQS"],
                   f=lambda e, c=c, X=X: e.tensor_tensor(out=XQS[:], in0=X[:, c, :],
                                                         in1=DHP[:], op=ALU.subtract)))
            th.append(lambda c=c: ctx.op("pool", r=["XQS"], w=["xqh"], dma="xq",
                   f=lambda e, c=c, t0=t0: e.dma_start(
                       out=xqh[c * 128:(c + 1) * 128, t0:t0 + TT], in_=XQS[:])))
        return th

    p1_A(0)
    p1_B(0)
    pend = []
    for it in range(NT):
        p1_CD(it, pend)
        if it + 1 < NT:
            p1_A(it + 1)
        p1_E(it)
        p1_y1s(it)
        if it + 1 < NT:
            p1_B(it + 1)
        pend = p1_F(it)
    for f in pend:
        f()

    ctx.barrier()
    s1.close()

    s2 = contextlib.ExitStack()
    f1q = sbA(s2, "f1q", [128, NCHUNK, F2], BF16)
    f2q = sbA(s2, "f2q", [128, F2CHUNK, D], BF16)
    SWB = sbA(s2, "SWB", [1, 2], F32)
    THR = sbA(s2, "THR", [128, 8], F32)
    B1S = sbA(s2, "B1S", [128, F2CHUNK], F32)
    B2S = sbA(s2, "B2S", [128, NCHUNK], F32)
    ROW2 = sbA(s2, "ROW2", [1, 4, TT], F32)
    s2w = contextlib.ExitStack()
    WQ0 = sbA(s2w, "WQ0", [128, F2], F32)
    WQ1 = sbA(s2w, "WQ1", [128, F2], F32)
    WQB = sbA(s2w, "WQB", [128, F2], BF16)
    WQM = sbA(s2w, "WQM", [128, F2], BF16)

    psH0 = psA(s2, "psH0", [128, TT])
    psH1 = psA(s2, "psH1", [128, TT])
    psH2 = psA(s2, "psH2", [128, TT])
    psRowH = psA(s2, "psRowH", [1, 1, TT])
    psO = psA(s2, "psO", [128, NCHUNK, TT])
    kO2 = lambda dc: "po:%d" % (dc // 2)

    ctx.op("sp", w=["B1S"], dma="vl",
           f=lambda e: e.dma_start(out=B1S[:], in_=b1d[:]))
    ctx.op("sp", w=["B2S"], dma="vl",
           f=lambda e: e.dma_start(out=B2S[:], in_=b2d[:]))

    for wi, (wd, nch) in enumerate(((f1Td, NCHUNK), (f2Td, F2CHUNK))):
        ncols = F2 if wi == 0 else D
        nblk = ncols // TT
        for c in range(nch):
            WQ = WQ0 if c % 2 == 0 else WQ1
            kW = "WQ%d" % (c % 2)
            ctx.op("sp", w=[kW], dma="wq",
                   f=lambda e, wd=wd, c=c, WQ=WQ, ncols=ncols: e.dma_start(
                       out=WQ[:, 0:ncols], in_=wd[c * 128:(c + 1) * 128, :]))
            ctx.op("dve", r=[kW], w=["WQB"],
                   f=lambda e, WQ=WQ, ncols=ncols: e.scalar_tensor_tensor(
                       out=WQB[:, 0:ncols], in0=WQ[:, 0:ncols], scalar=-1.0,
                       in1=WQ[:, 0:ncols], op0=ALU.mult, op1=ALU.max))
            def emit_ws(e, c=c, nch=nch, nblk=nblk):
                for j in range(nblk):
                    mm = e.matmul(psRowH[:, 0, :], ONEB[:], WQB[:, j * TT:(j + 1) * TT],
                                  start=(c == 0 and j == 0),
                                  stop=(c == nch - 1 and j == nblk - 1))
                return mm
            ctx.op("pe", r=["WQB", "onesbf"], w=["ph:row"], f=emit_ws)
        ctx.op("dve", r=["ph:row"], w=["rows2"],
               f=lambda e: e.tensor_copy(out=ROW2[:, 0, :], in_=psRowH[:, 0, :]))
        ctx.op("dve", r=["rows2"], w=["rows2"],
               f=lambda e: e.tensor_reduce(out=ROW2[:, 1, 0:1], in_=ROW2[:, 0, :],
                                           axis=AX.X, op=ALU.add))
        ctx.op("dve", r=["rows2"], w=["SWB"],
               f=lambda e, wi=wi: e.tensor_scalar(out=SWB[:, wi:wi + 1],
                                                  in0=ROW2[:, 1, 0:1],
                                                  scalar1=1.0 / (D * F2), scalar2=1e-5,
                                                  op0=ALU.mult, op1=ALU.max))
    ctx.op("dve", r=["SWB"], w=["rows2"],
           f=lambda e: e.tensor_scalar(out=ROW2[:, 2, 0:2], in0=SWB[:],
                                       scalar1=0.5, scalar2=None, op0=ALU.mult))
    ctx.op("dve", r=["rows2"], w=["rows2"],
           f=lambda e: e.tensor_scalar(out=ROW2[:, 2, 2:4], in0=ROW2[:, 2, 0:2],
                                       scalar1=-1.0, scalar2=None, op0=ALU.mult))
    ctx.op("dve", r=["SWB"], w=["rows2"],
           f=lambda e: e.tensor_copy(out=ROW2[:, 2, 4:6], in_=SWB[:]))
    ctx.op("dve", w=["rows2"],
           f=lambda e: e.memset(ROW2[:, 2, 6:8], 0.0))
    ctx.op("dve", r=["rows2"], w=["rows2"],
           f=lambda e: e.tensor_copy(out=ROW2[:, 3, 0:4],
                                     in_=bass.AP(ROW2, 2 * TT, [[4 * TT, 1], [1, 2], [2, 2]])))
    ctx.op("sp", r=["rows2"], w=["thrh"], dma="vl",
           f=lambda e: e.dma_start(out=thrh_ap[:, 0:4], in_=ROW2[:, 3, 0:4]))
    ctx.op("sp", r=["rows2"], w=["thrh"], dma="vl",
           f=lambda e: e.dma_start(out=thrh_ap[:, 4:8], in_=ROW2[:, 2, 4:8]))
    ctx.op("sp", r=["thrh"], w=["THR"], dma="vl",
           f=lambda e: e.dma_start(out=THR[:], in_=bass.AP(thrh, 0, [[0, 128], [1, 8]])))
    for wi, (wd, nch, dst, kd) in enumerate(((f1Td, NCHUNK, f1q, "f1q"),
                                             (f2Td, F2CHUNK, f2q, "f2q"))):
        ncols = F2 if wi == 0 else D
        for c in range(nch):
            WQ = WQ0 if c % 2 == 0 else WQ1
            WQo = WQ1 if c % 2 == 0 else WQ0
            kW = "WQ%d" % (c % 2)
            kO = "WQ%d" % ((c + 1) % 2)
            ctx.op("sp", w=[kW], dma="wq",
                   f=lambda e, wd=wd, c=c, WQ=WQ, ncols=ncols: e.dma_start(
                       out=WQ[:, 0:ncols], in_=wd[c * 128:(c + 1) * 128, :]))
            ctx.op("dve", r=[kW, "THR"], w=["WQM"],
                   f=lambda e, WQ=WQ, wi=wi, ncols=ncols: e.tensor_scalar(
                       out=WQM[:, 0:ncols], in0=WQ[:, 0:ncols],
                       scalar1=THR[:, 2 * wi:2 * wi + 1], scalar2=None, op0=ALU.is_gt))
            ctx.op("dve", r=[kW, "THR"], w=[kW],
                   f=lambda e, WQ=WQ, wi=wi, ncols=ncols: e.tensor_scalar(
                       out=WQ[:, 0:ncols], in0=WQ[:, 0:ncols],
                       scalar1=THR[:, 2 * wi + 1:2 * wi + 2], scalar2=None, op0=ALU.is_lt))
            ctx.op("dve", r=[kW, "WQM"], w=[kd],
                   f=lambda e, WQ=WQ, dst=dst, c=c, ncols=ncols: e.tensor_tensor(
                       out=dst[:, c, :], in0=WQM[:, 0:ncols], in1=WQ[:, 0:ncols],
                       op=ALU.subtract))
    ctx.barrier()
    s2w.close()

    HHT = sbA(s2, "HHT", [128, 40, TT], FP16)
    HQT = sbA(s2, "HQT", [128, 40, TT], BF16)
    hslot = lambda it, fc: (F2CHUNK * it + fc) % 40
    XQ0 = sbA(s2, "XQ0", [128, NCHUNK, TT], BF16)
    XQ1 = sbA(s2, "XQ1", [128, NCHUNK, TT], BF16)
    XO0 = sbA(s2, "XO0", [128, NCHUNK, TT], F32)
    XO1 = sbA(s2, "XO1", [128, NCHUNK, TT], F32)
    HS0 = sbA(s2, "HS0", [128, TT], BF16)
    HS1 = sbA(s2, "HS1", [128, TT], BF16)
    PR0 = sbA(s2, "PR0", [128, TT], F32)
    PR1 = sbA(s2, "PR1", [128, TT], F32)
    HI4 = sbA(s2, "HI4", [128, 2, TT], I32)
    Y1H = sbA(s2, "Y1H", [128, 2, TT], FP16)
    DH = sbA(s2, "DH", [128, 2, TT], FP16)
    MH = sbA(s2, "MH", [128, 2, TT], FP16)
    G2BH = sbA(s2, "G2BH", [128, TT], FP16)
    MX2 = sbA(s2, "MX2", [128, TT], FP16)
    COL2 = sbA(s2, "COL2", [128, 2], F32)
    IABC = sbA(s2, "IABC", [128, TT], FP16)
    IHB = sbA(s2, "IHB", [128, TT], F32)

    def p2_loads(it):
        t0 = it * TT
        tsl = slice(t0, t0 + TT)
        XQ = XQ0 if it % 2 == 0 else XQ1
        kXQ = "XQ%d" % (it % 2)
        ctx.op("sp", r=["xqh"], w=[kXQ], dma="xl",
               f=lambda e, XQ=XQ, tsl=tsl: e.dma_start(out=XQ[:], in_=rearr(xqh)[:, :, tsl]))
        ctx.op("sp", r=["iah"], w=["rows2"], dma="ia",
               f=lambda e, tsl=tsl: e.dma_start(out=ROW2[:, 3, :], in_=iah[0:1, tsl]))
        bcast(ROW2[:, 3, :], ["rows2"], IABC, "IABC", psH0, "ph:0")

    def p2_load_xo(it):
        t0 = it * TT
        tsl = slice(t0, t0 + TT)
        XO = XO0 if it % 2 == 0 else XO1
        kXO = "XO%d" % (it % 2)
        ctx.op("sp", r=["xoh"], w=[kXO], dma="x2",
               f=lambda e, XO=XO, tsl=tsl: e.dma_start(out=XO[:], in_=rearr(xoh)[:, :, tsl]))

    def p2_m6_chunk(it, fc):
        XQ = XQ0 if it % 2 == 0 else XQ1
        kXQ = "XQ%d" % (it % 2)
        pH = (psH0, psH1, psH2)[fc % 3]
        kH = "ph:%d" % (fc % 3)
        hs_ = hslot(it, fc)
        kHc = "HHT:%d" % hs_
        def emit_m6(e, fc=fc, pH=pH, XQ=XQ):
            for c in range(NCHUNK):
                mm = e.matmul(pH[:], f1q[:, c, fc * 128:(fc + 1) * 128], XQ[:, c, :],
                              start=(c == 0), stop=(c == NCHUNK - 1))
            return mm
        ctx.op("pe", r=["f1q", kXQ], w=[kH], f=emit_m6)
        PR = PR0 if fc % 2 == 0 else PR1
        kPR = "PR%d" % (fc % 2)
        ctx.op("dve", r=[kH, "IABC"], w=[kPR],
               f=lambda e, pH=pH, PR=PR: e.tensor_tensor(out=PR[:], in0=pH[:],
                                                         in1=IABC[:], op=ALU.mult))
        ctx.op("act", r=[kPR, "THR", "B1S"], w=[kHc],
               f=lambda e, fc=fc, PR=PR, hs_=hs_: e.activation(
                   out=HHT[:, hs_, :], in_=PR[:], func=AF.Gelu,
                   bias=B1S[:, fc:fc + 1], scale=THR[:, 4:5]))
        hsq = HS0 if fc % 2 == 0 else HS1
        khs = "HS%d" % (fc % 2)
        ctx.op("act", r=[kHc], w=[khs],
               f=lambda e, hs_=hs_, hsq=hsq: e.activation(out=hsq[:], in_=HHT[:, hs_, :],
                                                          func=AF.Square))
        ctx.op("pe", r=[khs, "onesbf"], w=["ph:row"],
               f=lambda e, fc=fc, hsq=hsq: e.matmul(psRowH[:, 0, :], ONEB[:], hsq[:],
                                                    start=(fc == 0), stop=(fc == F2CHUNK - 1)))
        if fc == 0:
            ctx.op("dve", r=[kHc], w=["MX2"],
                   f=lambda e, hs_=hs_: e.tensor_copy(out=MX2[:], in_=HHT[:, hs_, :]))
            ctx.op("dve", r=[kHc, "MX2"], w=["MX2"],
                   f=lambda e, hs_=hs_: e.scalar_tensor_tensor(out=MX2[:], in0=HHT[:, hs_, :],
                                                               scalar=-1.0, in1=MX2[:],
                                                               op0=ALU.mult, op1=ALU.max))
        else:
            ctx.op("dve", r=[kHc, "MX2"], w=["MX2"],
                   f=lambda e, hs_=hs_: e.tensor_tensor(out=MX2[:], in0=MX2[:],
                                                        in1=HHT[:, hs_, :], op=ALU.max))
            ctx.op("dve", r=[kHc, "MX2"], w=["MX2"],
                   f=lambda e, hs_=hs_: e.scalar_tensor_tensor(out=MX2[:], in0=HHT[:, hs_, :],
                                                               scalar=-1.0, in1=MX2[:],
                                                               op0=ALU.mult, op1=ALU.max))

    def p2_stats(it, pend=None):
        pend = list(pend or [])

        def drain(n=1):
            for _ in range(min(n, len(pend))):
                pend.pop(0)()
        ctx.op("dve", r=["ph:row"], w=["rows2"],
               f=lambda e: e.tensor_scalar(out=ROW2[:, 0, :], in0=psRowH[:, 0, :],
                                           scalar1=1.0 / F2, scalar2=EPS_BL,
                                           op0=ALU.mult, op1=ALU.add))
        ctx.op("dve", r=["rows2"], w=["rows2"],
               f=lambda e: e.reciprocal(out=ROW2[:, 1, :], in_=ROW2[:, 0, :]))
        drain(2)
        ctx.op("act", r=["rows2"], w=["rows2"],
               f=lambda e: e.activation(out=ROW2[:, 0, :], in_=ROW2[:, 1, :], func=AF.Sqrt))
        drain(2)
        ctx.op("dve", r=["MX2"], w=["PR0"],
               f=lambda e: e.tensor_copy(out=PR0[:], in_=MX2[:]))
        drain(2)
        for k in range(TT // 128):
            pT = psH0 if k % 2 == 0 else psH1
            kT = "ph:%d" % (k % 2)
            ctx.op("pe", r=["PR0", "IDF"], w=[kT],
                   f=lambda e, k=k, pT=pT: e.transpose(pT[:, 0:128],
                                                       PR0[:, k * 128:(k + 1) * 128],
                                                       IDF[:]))
            ctx.op("dve", r=[kT], w=["COL2"],
                   f=lambda e, k=k, pT=pT: e.tensor_reduce(out=COL2[:, k:k + 1],
                                                           in_=pT[:, 0:128], axis=AX.X,
                                                           op=ALU.max))
            ctx.op("pe", r=["COL2", "IDF"], w=["ph:row"],
                   f=lambda e, k=k: e.transpose(psRowH[:, 0, k * 128:(k + 1) * 128],
                                                COL2[:, k:k + 1], IDF[:]))
            drain(3)
        ctx.op("dve", r=["rows2", "ph:row"], w=["rows2"],
               f=lambda e: e.tensor_tensor(out=ROW2[:, 1, :], in0=ROW2[:, 0, :],
                                           in1=psRowH[:, 0, :], op=ALU.mult))
        ctx.op("dve", r=["rows2"], w=["rows2"],
               f=lambda e: e.tensor_scalar(out=ROW2[:, 1, :], in0=ROW2[:, 1, :],
                                           scalar1=1e-5, scalar2=None, op0=ALU.max))
        ctx.op("dve", r=["rows2"], w=["rows2"],
               f=lambda e: e.reciprocal(out=ROW2[:, 2, :], in_=ROW2[:, 1, :]))
        ctx.op("dve", r=["rows2"], w=["rows2"],
               f=lambda e: e.tensor_tensor(out=ROW2[:, 2, :], in0=ROW2[:, 0, :],
                                           in1=ROW2[:, 2, :], op=ALU.mult))
        ctx.op("dve", r=["rows2"], w=["rows2"],
               f=lambda e: e.tensor_scalar(out=ROW2[:, 2, :], in0=ROW2[:, 2, :],
                                           scalar1=127.0, scalar2=None, op0=ALU.mult))
        ctx.op("dve", r=["rows2"], w=["rows2"],
               f=lambda e: e.tensor_scalar(out=ROW2[:, 1, :], in0=ROW2[:, 1, :],
                                           scalar1=1.0 / 127.0, scalar2=None, op0=ALU.mult))
        drain(2)
        ctx.op("dve", r=["rows2", "SWB"], w=["rows2"],
               f=lambda e: e.tensor_tensor(out=ROW2[:, 1, :], in0=ROW2[:, 1, :],
                                           in1=row01(SWB, TT, off=1, rowlen=2),
                                           op=ALU.mult))
        bcast(ROW2[:, 2, :], ["rows2"], G2BH, "G2BH", psH1, "ph:1")
        drain(len(pend))

    def p2_pair_head(it, g):
        s0 = hslot(it, 2 * g)
        sl = slice(s0, s0 + 2)
        kH0, kH1 = "HHT:%d" % s0, "HHT:%d" % (s0 + 1)
        ctx.op("dve", r=[kH0, kH1, "G2BH"], w=["Y1H"],
               f=lambda e, sl=sl: e.tensor_tensor(out=Y1H[:], in0=HHT[:, sl, :],
                                                  in1=bcast_ap(G2BH, 2, TT),
                                                  op=ALU.mult))
        ctx.op("act", r=["Y1H"], w=["HI4"],
               f=lambda e: e.activation(out=HI4[:], in_=Y1H[:], func=AF.Copy))
        ctx.op("act", r=["HI4"], w=["DH"],
               f=lambda e: e.activation(out=DH[:], in_=HI4[:], func=AF.Copy,
                                        bias=-8.0))

    def p2_pair_tail(it, g):
        s0 = hslot(it, 2 * g)
        ctx.op("dve", r=["Y1H", "DH"], w=["DH"],
               f=lambda e: e.tensor_tensor(out=DH[:], in0=Y1H[:], in1=DH[:],
                                           op=ALU.subtract))
        for s in (8.0, 4.0, 2.0, 1.0):
            ctx.op("dve", r=["DH"], w=["MH"],
                   f=lambda e, s=s: e.tensor_scalar(out=MH[:], in0=DH[:],
                                                    scalar1=s - 0.5, scalar2=None,
                                                    op0=ALU.is_gt))
            ctx.op("dve", r=["DH", "MH"], w=["DH"],
                   f=lambda e, s=s: e.scalar_tensor_tensor(out=DH[:], in0=MH[:],
                                                           scalar=-s, in1=DH[:],
                                                           op0=ALU.mult, op1=ALU.add))
        ctx.op("dve", r=["Y1H", "DH"],
               w=["HQTs:%d" % s0, "HQTs:%d" % (s0 + 1)],
               f=lambda e, s0=s0: e.tensor_tensor(out=HQT[:, s0:s0 + 2, :], in0=Y1H[:],
                                                  in1=DH[:], op=ALU.subtract))

    def p2_m7_mms(it):
        hq_keys = ["HQTs:%d" % hslot(it, fc) for fc in range(F2CHUNK)]
        th = []
        for dc in range(NCHUNK):
            def emit_m7(e, dc=dc):
                for fc in range(F2CHUNK):
                    mm = e.matmul(psO[:, dc, :], f2q[0:128, fc, dc * 128:(dc + 1) * 128],
                                  HQT[:, hslot(it, fc), :],
                                  start=(fc == 0), stop=(fc == F2CHUNK - 1))
                return mm
            th.append(lambda emit_m7=emit_m7, dc=dc: ctx.op(
                "pe", r=["f2q"] + hq_keys, w=[kO2(dc)], f=emit_m7))
        return th

    def p2_invh_bcast(it):
        bcast(ROW2[:, 1, :], ["rows2"], IHB, "IHB", psH1, "ph:1")

    def p2_m7_evac(it):
        t0 = it * TT
        tsl = slice(t0, t0 + TT)
        XO = XO0 if it % 2 == 0 else XO1
        kXO = "XO%d" % (it % 2)
        th = []
        for dc in range(NCHUNK):
            th.append(lambda dc=dc: ctx.op(
                "dve", r=[kO2(dc), "IHB"], w=["PR1"],
                f=lambda e, dc=dc: e.tensor_tensor(out=PR1[:], in0=psO[:, dc, :],
                                                   in1=IHB[:], op=ALU.mult)))
            th.append(lambda dc=dc, XO=XO: ctx.op(
                "dve", r=["PR1", "B2S", kXO], w=[kXO],
                f=lambda e, dc=dc, XO=XO: e.scalar_tensor_tensor(
                    out=XO[:, dc, :], in0=PR1[:], scalar=B2S[:, dc:dc + 1],
                    in1=XO[:, dc, :], op0=ALU.add, op1=ALU.add)))
        th.append(lambda XO=XO: ctx.op(
            "pool", r=[kXO], w=["outh"], dma="fo",
            f=lambda e, tsl=tsl, XO=XO: e.dma_start(out=rearr(outT)[:, :, tsl], in_=XO[:])))
        return th

    p2_loads(0)
    p2_load_xo(0)
    for fc in range(F2CHUNK):
        p2_m6_chunk(0, fc)
    for it in range(NT):
        mm_th = p2_m7_mms(it - 1) if it > 0 else []
        ev_th = p2_m7_evac(it - 1) if it > 0 else []
        p2_stats(it, [])
        if it + 1 < NT:
            p2_loads(it + 1)
            for fc in range(4):
                p2_m6_chunk(it + 1, fc)
        for g in range(F2CHUNK // 2):
            if it + 1 < NT and 2 * g + 5 < F2CHUNK:
                p2_m6_chunk(it + 1, 2 * g + 4)
                p2_m6_chunk(it + 1, 2 * g + 5)
            p2_pair_head(it, g)
            if it > 0:
                if g < 4:
                    mm_th[2 * g]()
                    mm_th[2 * g + 1]()
                if 4 <= g < 12:
                    ev_th[2 * (g - 4)]()
                    ev_th[2 * (g - 4) + 1]()
                if g == 12:
                    ev_th[16]()
            p2_pair_tail(it, g)
            if g == 13:
                p2_invh_bcast(it)
                if it + 1 < NT:
                    p2_load_xo(it + 1)
    for f in p2_m7_mms(NT - 1):
        f()
    for f in p2_m7_evac(NT - 1):
        f()

    ctx.barrier()
    s2.close()

    sem = {s: es.enter_context(nc.semaphore(s.replace(":", "_")))
           for s in ctx.cnt if ctx.cnt[s] > 0}
    with nc.Block() as block:
        def mk(engname):
            ops = ctx.q[engname]
            def body(e):
                for o in ops:
                    for s, v in o.waits:
                        e.wait_ge(sem[s], v)
                    if o.emit is not None:
                        instr = o.emit(e)
                        instr.then_inc(sem[o.sem], o.inc)
            return body
        block.tensor(mk("pe"))
        block.vector(mk("dve"))
        block.scalar(mk("act"))
        block.gpsimd(mk("pool"))
        block.sync(mk("sp"))
    es.close()
    return nc


def _host_inputs(inputs):
    f32c = lambda a: np.ascontiguousarray(np.asarray(a, dtype=np.float32))
    bw = f32c(inputs["L0_attn_B_w"])
    cw = f32c(inputs["L0_attn_C_w"])
    w1 = f32c(inputs["L0_attn_w1"])
    w2 = f32c(inputs["L0_attn_w2"])
    w3 = f32c(inputs["L0_attn_w3"])
    f1 = f32c(inputs["L0_ffn_w1"])
    f2 = f32c(inputs["L0_ffn_w2"])
    w2T = np.zeros((22 * 128, D), dtype=np.float32)
    w2T[:F1, :] = w2.T
    base = {
        "ll": np.ascontiguousarray(f32c(inputs["L0_attn_log_lambda"]).reshape(2, 128).T),
        "bwT": np.ascontiguousarray(bw.T),
        "cwT": np.ascontiguousarray(cw.T),
        "w1T": np.ascontiguousarray(w1.T),
        "w3T": np.ascontiguousarray(w3.T),
        "w2T": np.ascontiguousarray(w2T),
        "dsk": np.ascontiguousarray(f32c(inputs["L0_attn_D_skip"]).reshape(NCHUNK, 128).T),
        "snw": np.ascontiguousarray(f32c(inputs["L0_attn_ssm_norm_w"]).reshape(NCHUNK, 128).T),
        "fnw": np.ascontiguousarray(f32c(inputs["L0_attn_ffn_norm_w"]).reshape(NCHUNK, 128).T),
        "ident": np.eye(128, dtype=np.float32),
        "f1T": np.ascontiguousarray(f1.T),
        "b1": np.ascontiguousarray(f32c(inputs["L0_ffn_b1"]).reshape(F2CHUNK, 128).T),
        "f2T": np.ascontiguousarray(f2.T),
        "b2": np.ascontiguousarray(f32c(inputs["L0_ffn_b2"]).reshape(NCHUNK, 128).T),
    }
    return base


def kernel(**inputs):
    x = np.asarray(inputs["L0_attn_x"], dtype=np.float32)
    B = x.shape[0]
    nc = build(T=x.shape[1], TT=256)
    base = _host_inputs(inputs)
    in_maps = []
    for c in range(B):
        m = dict(base)
        m["xT"] = np.ascontiguousarray(x[c].T)
        in_maps.append(m)
    res = run_bass_kernel_spmd(nc, in_maps, core_ids=list(range(B)))
    out = np.stack([np.asarray(r["outT"]).T for r in res.results], axis=0)
    return np.ascontiguousarray(out, dtype=np.float32)


# version 30
# speedup vs baseline: 1.0092x; 1.0092x over previous
import contextlib
import numpy as np

import concourse.bass as bass
import concourse.mybir as mybir
from concourse.bass_utils import run_bass_kernel_spmd

F32 = mybir.dt.float32
BF16 = mybir.dt.bfloat16
FP16 = mybir.dt.float16
I32 = mybir.dt.int32
AF = mybir.ActivationFunctionType
ALU = mybir.AluOpType
AX = mybir.AxisListType

D = 1024
NST = 256
F1 = 2736
F2 = 4096
NCHUNK = D // 128
F2CHUNK = F2 // 128
EPS_LN = 1e-5
EPS_S = 1e-6
EPS_BL = 1e-5

FSL = [(i * 128, 128) for i in range(21)] + [(2688, 48)]

ENGS = ("pe", "dve", "act", "pool", "sp")


class Op:
    __slots__ = ("eng", "waits", "emit", "sem", "inc")

    def __init__(self, eng, waits, emit, sem, inc):
        self.eng, self.waits, self.emit, self.sem, self.inc = eng, waits, emit, sem, inc


class Ctx:

    def __init__(self, relax=True):
        self.relax = relax
        self.q = {e: [] for e in ENGS}
        self.cnt = {s: 0 for s in ENGS}
        self.waited = {e: {} for e in ENGS}
        self.writer = {}
        self.readers = {}

    def op(self, eng, r=(), w=(), f=None, dma=None):
        if dma is not None:
            sem = "D:" + dma
            inc = 16
            self.cnt.setdefault(sem, 0)
        else:
            sem = eng
            inc = 1
        need = {}
        if dma is not None and self.cnt[sem] > 0:
            need[sem] = self.cnt[sem]
        for k in r:
            st = self.writer.get(k)
            if st is not None:
                need[st[0]] = max(need.get(st[0], 0), st[1])
        for k in w:
            st = self.writer.get(k)
            if st is not None:
                need[st[0]] = max(need.get(st[0], 0), st[1])
            for s, v in self.readers.get(k, {}).items():
                need[s] = max(need.get(s, 0), v)
        waits = []
        wd = self.waited[eng]
        for s, v in need.items():
            if wd.get(s, 0) < v:
                waits.append((s, v))
                wd[s] = v
        self.cnt[sem] += inc
        stamp = (sem, self.cnt[sem])
        for k in w:
            self.writer[k] = stamp
            self.readers[k] = {}
        for k in r:
            d = self.readers.setdefault(k, {})
            d[stamp[0]] = max(d.get(stamp[0], 0), stamp[1])
        self.q[eng].append(Op(eng, waits, f, sem, inc))

    def barrier(self):
        tgt = dict(self.cnt)
        for e in ENGS:
            waits = []
            wd = self.waited[e]
            for s, v in tgt.items():
                if v > 0 and wd.get(s, 0) < v:
                    waits.append((s, v))
                    wd[s] = v
            if waits:
                self.q[e].append(Op(e, waits, None, None, 0))
        self.writer = {}
        self.readers = {}


def bcast_ap(t, chunks, tt):
    return bass.AP(t, 0, [[tt, 128], [0, chunks], [1, tt]])


def row01(t, n, off=0, rowlen=1):
    return bass.AP(t, off, [[rowlen, 1], [0, n]])


def build(T=4096, TT=256, relax=True):
    NT = T // TT
    nc = bass.Bass("TRN2", target_bir_lowering=False)

    dram = lambda name, shape, dt, kind: nc.dram_tensor(name, shape, dt, kind=kind).ap()
    xT = dram("xT", [D, T], F32, "ExternalInput")
    ll = dram("ll", [128, 2], F32, "ExternalInput")
    bwT = dram("bwT", [D, NST], F32, "ExternalInput")
    cwT = dram("cwT", [NST, D], F32, "ExternalInput")
    w1T = dram("w1T", [D, F1], F32, "ExternalInput")
    w3T = dram("w3T", [D, F1], F32, "ExternalInput")
    w2T = dram("w2T", [22 * 128, D], F32, "ExternalInput")
    dsk = dram("dsk", [128, NCHUNK], F32, "ExternalInput")
    snw = dram("snw", [128, NCHUNK], F32, "ExternalInput")
    fnw = dram("fnw", [128, NCHUNK], F32, "ExternalInput")
    identd = dram("ident", [128, 128], F32, "ExternalInput")
    f1Td = dram("f1T", [D, F2], F32, "ExternalInput")
    b1d = dram("b1", [128, F2CHUNK], F32, "ExternalInput")
    f2Td = dram("f2T", [F2, D], F32, "ExternalInput")
    b2d = dram("b2", [128, NCHUNK], F32, "ExternalInput")
    outT = dram("outT", [D, T], F32, "ExternalOutput")
    xqh = dram("xqh", [D, T], BF16, "Internal")
    xoh = dram("xoh", [D, T], F32, "Internal")
    iah = dram("iah", [1, T], F32, "Internal")
    thrh = nc.dram_tensor("thrh", [1, 8], F32, kind="Internal")
    thrh_ap = thrh.ap()

    rearr = lambda ap: ap.rearrange("(c p) t -> p c t", p=128)

    es = contextlib.ExitStack()
    ctx = Ctx(relax=relax)
    sbA = lambda st, name, shape, dt: st.enter_context(nc.sbuf_tensor(name, shape, dt))
    psA = lambda st, name, shape: st.enter_context(nc.psum_tensor(name, shape, F32))

    IDF = sbA(es, "IDF", [128, 128], F32)
    ONESR = sbA(es, "ONESR", [1, 128], F32)
    ONEB = sbA(es, "ONEB", [128, 1], BF16)

    ctx.op("sp", w=["IDF"], dma="vl",
           f=lambda e: e.dma_start(out=IDF[:], in_=identd[:]))
    ctx.op("dve", w=["ONESR"], f=lambda e: e.memset(ONESR[:], 1.0))
    ctx.op("dve", w=["onesbf"], f=lambda e: e.memset(ONEB[:], 1.0))

    def bcast(row_ap, row_keys, dst, kdst, pgrp, kpsum):
        ctx.op("pe", r=list(row_keys) + ["ONESR"], w=[kpsum],
               f=lambda e: e.matmul(pgrp[:, 0:TT], ONESR[:], row_ap,
                                    start=True, stop=True))
        ctx.op("act", r=[kpsum], w=[kdst],
               f=lambda e: e.activation(out=dst[:], in_=pgrp[:, 0:TT], func=AF.Copy))

    def round_chain(src_ap, src_keys, iq, kiq, dbuf, kd, mask_ap, kmask, emit_out):
        ctx.op("dve", r=list(src_keys), w=[kiq],
               f=lambda e: e.tensor_copy(out=iq, in_=src_ap))
        ctx.op("dve", r=list(src_keys) + [kiq], w=[kd],
               f=lambda e: e.scalar_tensor_tensor(out=dbuf, in0=iq, scalar=-1.0,
                                                  in1=src_ap, op0=ALU.mult, op1=ALU.add))
        ctx.op("dve", r=[kd], w=[kd],
               f=lambda e: e.tensor_scalar(out=dbuf, in0=dbuf, scalar1=8.0,
                                           scalar2=None, op0=ALU.add))
        for s in (8.0, 4.0, 2.0, 1.0):
            ctx.op("dve", r=[kd], w=[kmask],
                   f=lambda e, s=s: e.tensor_scalar(out=mask_ap, in0=dbuf,
                                                    scalar1=s - 0.5, scalar2=None,
                                                    op0=ALU.is_gt))
            ctx.op("dve", r=[kd, kmask], w=[kd],
                   f=lambda e, s=s: e.scalar_tensor_tensor(out=dbuf, in0=mask_ap,
                                                           scalar=-s, in1=dbuf,
                                                           op0=ALU.mult, op1=ALU.add))
        emit_out()

    s1 = contextlib.ExitStack()
    bwsb = sbA(s1, "bwsb", [128, NCHUNK, NST], BF16)
    cwsb = sbA(s1, "cwsb", [128, 2, D], BF16)
    w1sb = sbA(s1, "w1sb", [128, NCHUNK, F1], BF16)
    w3sb = sbA(s1, "w3sb", [128, NCHUNK, F1], BF16)
    w2sb = sbA(s1, "w2sb", [128, 22, D], BF16)
    X0 = sbA(s1, "X0", [128, NCHUNK, TT], F32)
    X1 = sbA(s1, "X1", [128, NCHUNK, TT], F32)
    Xbf = sbA(s1, "Xbf", [128, NCHUNK, TT], BF16)
    S1 = sbA(s1, "S1b", [128, NCHUNK, TT], BF16)
    BU = sbA(s1, "BU", [128, 2, TT], F32)
    HSB = sbA(s1, "HSB", [128, 2, TT], BF16)
    STATE = sbA(s1, "STATE", [128, 2, 1], F32)
    XB2 = sbA(s1, "XB2", [128, NCHUNK, TT], F32)
    XOUT = sbA(s1, "XOUT", [128, NCHUNK, TT], F32)
    GV = sbA(s1, "GV", [128, 22, TT], BF16)
    GS0 = sbA(s1, "GS0", [128, TT], BF16)
    GS1 = sbA(s1, "GS1", [128, TT], BF16)
    SCA = sbA(s1, "SCA", [128, TT], F32)
    SCT = sbA(s1, "SCT", [128, TT], F32)
    XI1 = sbA(s1, "XI1", [128, TT], I32)
    DHP = sbA(s1, "DHP", [128, TT], FP16)
    MHP = sbA(s1, "MHP", [128, TT], FP16)
    GBH = sbA(s1, "GBH", [128, TT], FP16)
    XQS = sbA(s1, "XQS", [128, TT], BF16)
    COLS = sbA(s1, "COLS", [128, 2], F32)
    BC = [sbA(s1, "BC%d" % i, [128, TT], F32) for i in range(3)]
    ROWS = sbA(s1, "ROWS", [1, 6, TT], F32)
    C1N = sbA(s1, "C1N", [128, 2, 1], F32)
    DP = sbA(s1, "DP", [128, NCHUNK], F32)
    SNW = sbA(s1, "SNW", [128, NCHUNK], F32)
    FNW = sbA(s1, "FNW", [128, NCHUNK], F32)
    LLS = sbA(s1, "LLS", [128, 2], F32)

    psRow = psA(s1, "psRow", [1, 2, TT])
    PaA = psA(s1, "PaA", [128, TT])
    PaB = psA(s1, "PaB", [128, TT])
    PaC = psA(s1, "PaC", [128, TT])
    psF = psA(s1, "psF", [128, NCHUNK, TT])
    kF = lambda dc: "pb:F%d" % (dc // 2)

    ctx.op("sp", w=["LLS"], dma="vl",
           f=lambda e: e.dma_start(out=LLS[:], in_=ll[:]))
    ctx.op("sp", w=["DPr"], dma="vl",
           f=lambda e: e.dma_start(out=DP[:], in_=dsk[:]))
    ctx.op("sp", w=["SNW"], dma="vl",
           f=lambda e: e.dma_start(out=SNW[:], in_=snw[:]))
    ctx.op("sp", w=["FNW"], dma="vl",
           f=lambda e: e.dma_start(out=FNW[:], in_=fnw[:]))
    ctx.op("sp", w=["X0"], dma="vl",
           f=lambda e: e.dma_start(out=X0[:], in_=rearr(bwT)))
    ctx.op("pool", w=["cw"], dma="wc",
           f=lambda e: e.dma_start(out=cwsb[:], in_=cwT.rearrange("(c p) t -> p c t", p=128)))
    ctx.op("pool", w=["w1"], dma="wc",
           f=lambda e: e.dma_start(out=w1sb[:], in_=rearr(w1T)))
    ctx.op("pool", w=["w3"], dma="wc",
           f=lambda e: e.dma_start(out=w3sb[:], in_=rearr(w3T)))
    ctx.op("pool", w=["w2"], dma="wc",
           f=lambda e: e.dma_start(out=w2sb[:], in_=w2T.rearrange("(c p) t -> p c t", p=128)))

    for c in range(NCHUNK):
        ctx.op("act", r=["X0", "SNW"], w=["bw"],
               f=lambda e, c=c: e.activation(out=bwsb[:, c, :], in_=X0[:, c, :],
                                             func=AF.Copy, scale=SNW[:, c:c + 1]))
    for n in range(2):
        def emit_c1(e, n=n):
            for c in range(NCHUNK):
                mm = e.matmul(psF[:, n, 0:1], bwsb[:, c, n * 128:(n + 1) * 128], ONEB[:],
                              start=(c == 0), stop=(c == NCHUNK - 1))
            return mm
        ctx.op("pe", r=["bw", "onesbf"], w=[kF(n)], f=emit_c1)
    for n in range(2):
        ctx.op("dve", r=[kF(n)], w=["C1N"],
               f=lambda e, n=n: e.tensor_scalar(out=C1N[:, n, :], in0=psF[:, n, 0:1],
                                                scalar1=-1.0, scalar2=None, op0=ALU.mult))
    ctx.op("dve", r=["DPr", "SNW"], w=["DP"],
           f=lambda e: e.tensor_tensor(out=DP[:], in0=DP[:], in1=SNW[:], op=ALU.mult))
    ctx.op("act", r=["LLS"], w=["LLS"],
           f=lambda e: e.activation(out=LLS[:], in_=LLS[:], func=AF.Sigmoid))
    ctx.op("dve", w=["STATE"], f=lambda e: e.memset(STATE[:], 0.0))

    def emit_sum(e, src_, row):
        for c in range(NCHUNK):
            mm = e.matmul(psRow[:, row, :], ONEB[:], src_[:, c, :],
                          start=(c == 0), stop=(c == NCHUNK - 1))
        return mm

    def p1_A(it):
        t0 = it * TT
        tsl = slice(t0, t0 + TT)
        X = X0 if it % 2 == 0 else X1
        kX = "X%d" % (it % 2)
        ctx.op("sp", w=[kX], dma="xl",
               f=lambda e, X=X, tsl=tsl: e.dma_start(out=X[:], in_=rearr(xT)[:, :, tsl]))
        ctx.op("dve", r=[kX], w=["Xbf"],
               f=lambda e, X=X: e.tensor_copy(out=Xbf[:], in_=X[:]))
        ctx.op("act", r=[kX], w=["S1"],
               f=lambda e, X=X: e.activation(out=S1[:], in_=X[:], func=AF.Square))
        ctx.op("pe", r=["Xbf", "onesbf"], w=["pb:row"], f=lambda e: emit_sum(e, Xbf, 0))
        ctx.op("pe", r=["S1", "onesbf"], w=["pb:row"],
               f=lambda e: emit_sum(e, S1, 1))
        for n in range(2):
            def emit_bu(e, n=n):
                for c in range(NCHUNK):
                    mm = e.matmul((PaA if n == 0 else PaB)[:],
                                  bwsb[:, c, n * 128:(n + 1) * 128], Xbf[:, c, :],
                                  start=(c == 0), stop=(c == NCHUNK - 1))
                return mm
            ctx.op("pe", r=["bw", "Xbf"], w=["pb:" + ("A" if n == 0 else "B")], f=emit_bu)
        ctx.op("dve", r=["pb:row"], w=["rows"],
               f=lambda e: e.tensor_scalar(out=ROWS[:, 0, :], in0=psRow[:, 0, :],
                                           scalar1=1.0 / D, scalar2=None, op0=ALU.mult))
        ctx.op("dve", r=["pb:row"], w=["rows"],
               f=lambda e: e.tensor_scalar(out=ROWS[:, 1, :], in0=psRow[:, 1, :],
                                           scalar1=1.0 / D, scalar2=None, op0=ALU.mult))
        ctx.op("dve", r=["rows"], w=["rows"],
               f=lambda e: e.tensor_tensor(out=ROWS[:, 2, :], in0=ROWS[:, 0, :],
                                           in1=ROWS[:, 0, :], op=ALU.mult))
        ctx.op("dve", r=["rows"], w=["rows"],
               f=lambda e: e.tensor_tensor(out=ROWS[:, 2, :], in0=ROWS[:, 1, :],
                                           in1=ROWS[:, 2, :], op=ALU.subtract))
        ctx.op("dve", r=["rows"], w=["rows"],
               f=lambda e: e.tensor_scalar(out=ROWS[:, 3, :], in0=ROWS[:, 2, :],
                                           scalar1=EPS_LN, scalar2=None, op0=ALU.add))
        ctx.op("dve", r=["rows"], w=["rows"],
               f=lambda e: e.tensor_scalar(out=ROWS[:, 4, :], in0=ROWS[:, 2, :],
                                           scalar1=1.0 + EPS_S, scalar2=EPS_S * EPS_LN,
                                           op0=ALU.mult, op1=ALU.add))
        ctx.op("dve", r=["rows"], w=["rows"],
               f=lambda e: e.reciprocal(out=ROWS[:, 1:3, :], in_=ROWS[:, 3:5, :]))
        ctx.op("act", r=["rows"], w=["rows"],
               f=lambda e: e.activation(out=ROWS[:, 3:5, :], in_=ROWS[:, 1:3, :], func=AF.Sqrt))
        ctx.op("dve", r=["rows"], w=["rows"],
               f=lambda e: e.tensor_tensor(out=ROWS[:, 1, :], in0=ROWS[:, 4, :],
                                           in1=ROWS[:, 0, :], op=ALU.mult))
        for bi, ri in ((0, 4), (1, 1), (2, 3)):
            bcast(ROWS[:, ri, :], ["rows"], BC[bi], "BC%d" % bi, PaC, "pb:C")
        for n in range(2):
            kP = "pb:" + ("A" if n == 0 else "B")
            ctx.op("dve", r=[kP, "BC0"], w=["BU"],
                   f=lambda e, n=n: e.tensor_tensor(out=BU[:, n, :],
                                                    in0=(PaA if n == 0 else PaB)[:],
                                                    in1=BC[0][:], op=ALU.mult))
            ctx.op("dve", r=["BC1", "C1N", "BU"], w=["BU"],
                   f=lambda e, n=n: e.scalar_tensor_tensor(out=BU[:, n, :], in0=BC[1][:],
                                                           scalar=C1N[:, n, :], in1=BU[:, n, :],
                                                           op0=ALU.mult, op1=ALU.add))
            ctx.op("dve", r=["BU", "LLS", "STATE"], w=["BU"],
                   f=lambda e, n=n: e.tensor_tensor_scan(
                       out=BU[:, n, :], data0=bass.AP(LLS, n, [[2, 128], [0, TT]]),
                       data1=BU[:, n, :], initial=STATE[:, n, :],
                       op0=ALU.mult, op1=ALU.add))
            ctx.op("dve", r=["BU"], w=["STATE"],
                   f=lambda e, n=n: e.tensor_copy(out=STATE[:, n, :], in_=BU[:, n, TT - 1:TT]))
        bcast(ROWS[:, 0, :], ["rows"], BC[1], "BC1", PaC, "pb:C")
        ctx.op("dve", r=["BU"], w=["hs"],
               f=lambda e: e.tensor_copy(out=HSB[:], in_=BU[:]))

    def p1_B(it):
        X = X0 if it % 2 == 0 else X1
        kX = "X%d" % (it % 2)
        for dc in range(NCHUNK):
            kY = "pb:" + ("A" if dc % 2 == 0 else "B")
            PY = PaA if dc % 2 == 0 else PaB
            def emit_y(e, dc=dc, PY=PY):
                for n in range(2):
                    mm = e.matmul(PY[:], cwsb[:, n, dc * 128:(dc + 1) * 128], HSB[:, n, :],
                                  start=(n == 0), stop=(n == 1))
                return mm
            ctx.op("pe", r=["cw", "hs"], w=[kY], f=emit_y)
            ctx.op("dve", r=[kX, "BC1"], w=["SCT"],
                   f=lambda e, dc=dc, X=X: e.tensor_tensor(out=SCT[:], in0=X[:, dc, :],
                                                           in1=BC[1][:], op=ALU.subtract))
            ctx.op("dve", r=["BC0", "BC2", "DP"], w=["SCA"],
                   f=lambda e, dc=dc: e.scalar_tensor_tensor(out=SCA[:], in0=BC[0][:],
                                                             scalar=DP[:, dc:dc + 1],
                                                             in1=BC[2][:],
                                                             op0=ALU.mult, op1=ALU.add))
            ctx.op("dve", r=["SCT", "SCA"], w=["SCA"],
                   f=lambda e: e.tensor_tensor(out=SCA[:], in0=SCT[:], in1=SCA[:], op=ALU.mult))
            ctx.op("dve", r=["SCA", kY], w=["XB2"],
                   f=lambda e, dc=dc, PY=PY: e.tensor_tensor(out=XB2[:, dc, :], in0=SCA[:],
                                                             in1=PY[:], op=ALU.add))
        ctx.op("act", r=["XB2"], w=["S1"],
               f=lambda e: e.activation(out=S1[:], in_=XB2[:], func=AF.Square))
        ctx.op("pe", r=["S1", "onesbf"], w=["pb:row"], f=lambda e: emit_sum(e, S1, 0))
        ctx.op("dve", r=["pb:row"], w=["rows"],
               f=lambda e: e.tensor_scalar(out=ROWS[:, 5, :], in0=psRow[:, 0, :],
                                           scalar1=1.0 / D, scalar2=EPS_S,
                                           op0=ALU.mult, op1=ALU.add))
        ctx.op("dve", r=["rows"], w=["rows"],
               f=lambda e: e.reciprocal(out=ROWS[:, 2, :], in_=ROWS[:, 5, :]))
        ctx.op("act", r=["rows"], w=["rows"],
               f=lambda e: e.activation(out=ROWS[:, 5, :], in_=ROWS[:, 2, :], func=AF.Sqrt))
        bcast(ROWS[:, 5, :], ["rows"], BC[0], "BC0", PaC, "pb:C")
        for dc in range(NCHUNK):
            ctx.op("dve", r=["XB2", "FNW", "BC0"], w=["S1"],
                   f=lambda e, dc=dc: e.scalar_tensor_tensor(out=S1[:, dc, :],
                                                             in0=XB2[:, dc, :],
                                                             scalar=FNW[:, dc:dc + 1],
                                                             in1=BC[0][:],
                                                             op0=ALU.mult, op1=ALU.mult))

    def p1_CD(it, pend=None):
        pend = list(pend or [])

        def drain(n):
            for _ in range(min(n, len(pend))):
                pend.pop(0)()
        pa_slots = [(PaA, "pb:A"), (PaB, "pb:B"), (PaC, "pb:C")]
        for fs in range(22):
            off, fsz = FSL[fs]
            pg, kg = pa_slots[(2 * fs) % 3]
            pv, kv = pa_slots[(2 * fs + 1) % 3]
            gs = GS0 if fs % 2 == 0 else GS1
            kgs = "GS%d" % (fs % 2)
            def emit_g(e, off=off, fsz=fsz, pg=pg):
                for c in range(NCHUNK):
                    mm = e.matmul(pg[0:fsz, :], w1sb[:, c, off:off + fsz], S1[:, c, :],
                                  start=(c == 0), stop=(c == NCHUNK - 1))
                return mm
            ctx.op("pe", r=["w1", "S1"], w=[kg], f=emit_g)
            def emit_v(e, off=off, fsz=fsz, pv=pv):
                for c in range(NCHUNK):
                    mm = e.matmul(pv[0:fsz, :], w3sb[:, c, off:off + fsz], S1[:, c, :],
                                  start=(c == 0), stop=(c == NCHUNK - 1))
                return mm
            ctx.op("pe", r=["w3", "S1"], w=[kv], f=emit_v)
            ctx.op("act", r=[kg], w=[kgs],
                   f=lambda e, fsz=fsz, pg=pg, gs=gs: e.activation(
                       out=gs[0:fsz, :], in_=pg[0:fsz, :], func=AF.Silu))
            ctx.op("dve", r=[kgs, kv], w=["GV"],
                   f=lambda e, fs=fs, fsz=fsz, pv=pv, gs=gs: e.tensor_tensor(
                       out=GV[0:fsz, fs, :], in0=gs[0:fsz, :], in1=pv[0:fsz, :],
                       op=ALU.mult))
            drain(3)
        for dc in range(NCHUNK):
            def emit_f(e, dc=dc):
                for fs in range(22):
                    off, fsz = FSL[fs]
                    mm = e.matmul(psF[:, dc, :], w2sb[0:fsz, fs, dc * 128:(dc + 1) * 128],
                                  GV[0:fsz, fs, :], start=(fs == 0), stop=(fs == 21))
                return mm
            ctx.op("pe", r=["w2", "GV"], w=[kF(dc)], f=emit_f)
            drain(5)
        drain(len(pend))

    def p1_E(it):
        t0 = it * TT
        tsl = slice(t0, t0 + TT)
        X = X0 if it % 2 == 0 else X1
        kX = "X%d" % (it % 2)
        ctx.op("dve", r=[kX, "XB2"], w=[kX],
               f=lambda e, X=X: e.tensor_tensor(out=X[:], in0=X[:], in1=XB2[:], op=ALU.add))
        for dc in range(NCHUNK):
            ctx.op("dve", r=[kX, kF(dc)], w=["XOUT"],
                   f=lambda e, dc=dc, X=X: e.tensor_tensor(out=XOUT[:, dc, :],
                                                           in0=X[:, dc, :],
                                                           in1=psF[:, dc, :], op=ALU.add))
        ctx.op("pool", r=["XOUT"], w=["xoh"], dma="xo",
               f=lambda e, tsl=tsl: e.dma_start(out=rearr(xoh)[:, :, tsl], in_=XOUT[:]))
        ctx.op("dve", r=["XOUT", "GV"], w=["GV"],
               f=lambda e: e.tensor_copy(out=GV[:, 0:8, :], in_=XOUT[:]))
        ctx.op("act", r=["XOUT", "GV"], w=["GV"],
               f=lambda e: e.activation(out=GV[:, 8:16, :], in_=XOUT[:], func=AF.Square))
        ctx.op("pe", r=["GV", "onesbf"], w=["pb:row"],
               f=lambda e: emit_sum(e, GV[:, 0:8, :], 0))
        ctx.op("pe", r=["GV", "onesbf"], w=["pb:row"],
               f=lambda e: emit_sum(e, GV[:, 8:16, :], 1))
        ctx.op("dve", r=["pb:row"], w=["rows"],
               f=lambda e: e.tensor_scalar(out=ROWS[:, 0, :], in0=psRow[:, 0, :],
                                           scalar1=1.0 / D, scalar2=None, op0=ALU.mult))
        ctx.op("dve", r=["pb:row"], w=["rows"],
               f=lambda e: e.tensor_scalar(out=ROWS[:, 1, :], in0=psRow[:, 1, :],
                                           scalar1=1.0 / D, scalar2=None, op0=ALU.mult))
        ctx.op("dve", r=["rows"], w=["rows"],
               f=lambda e: e.tensor_tensor(out=ROWS[:, 2, :], in0=ROWS[:, 0, :],
                                           in1=ROWS[:, 0, :], op=ALU.mult))
        ctx.op("dve", r=["rows"], w=["rows"],
               f=lambda e: e.tensor_tensor(out=ROWS[:, 2, :], in0=ROWS[:, 1, :],
                                           in1=ROWS[:, 2, :], op=ALU.subtract))
        ctx.op("dve", r=["rows"], w=["rows"],
               f=lambda e: e.tensor_scalar(out=ROWS[:, 3, :], in0=ROWS[:, 2, :],
                                           scalar1=1.0 + EPS_BL, scalar2=EPS_BL * EPS_LN,
                                           op0=ALU.mult, op1=ALU.add))
        ctx.op("dve", r=["rows"], w=["rows"],
               f=lambda e: e.reciprocal(out=ROWS[:, 4, :], in_=ROWS[:, 3, :]))
        ctx.op("act", r=["rows"], w=["rows"],
               f=lambda e: e.activation(out=ROWS[:, 3, :], in_=ROWS[:, 4, :], func=AF.Sqrt))
        bcast(ROWS[:, 0, :], ["rows"], BU[:, 0, :], "BU", PaC, "pb:C")
        ctx.op("dve", r=["XOUT", "BU"], w=["XB2"],
               f=lambda e: e.tensor_tensor(out=XB2[:], in0=XOUT[:],
                                           in1=bass.AP(BU, 0, [[2 * TT, 128],
                                                               [0, NCHUNK], [1, TT]]),
                                           op=ALU.subtract))
        ctx.op("dve", r=["XB2"], w=["SCA"],
               f=lambda e: e.tensor_tensor(out=SCA[:], in0=XB2[:, 0, :], in1=XB2[:, 1, :],
                                           op=ALU.max))
        ctx.op("dve", r=["XB2"], w=["SCT"],
               f=lambda e: e.tensor_tensor(out=SCT[:], in0=XB2[:, 0, :], in1=XB2[:, 1, :],
                                           op=ALU.min))
        for c in range(2, NCHUNK):
            ctx.op("dve", r=["XB2", "SCA"], w=["SCA"],
                   f=lambda e, c=c: e.tensor_tensor(out=SCA[:], in0=SCA[:], in1=XB2[:, c, :],
                                                    op=ALU.max))
            ctx.op("dve", r=["XB2", "SCT"], w=["SCT"],
                   f=lambda e, c=c: e.tensor_tensor(out=SCT[:], in0=SCT[:], in1=XB2[:, c, :],
                                                    op=ALU.min))
        ctx.op("dve", r=["SCT", "SCA"], w=["SCA"],
               f=lambda e: e.scalar_tensor_tensor(out=SCA[:], in0=SCT[:], scalar=-1.0,
                                                  in1=SCA[:], op0=ALU.mult, op1=ALU.max))
        for k in range(TT // 128):
            pT = PaC
            kT = "pb:C"
            ctx.op("pe", r=["SCA", "IDF"], w=[kT],
                   f=lambda e, k=k, pT=pT: e.transpose(pT[:, 0:128],
                                                       SCA[:, k * 128:(k + 1) * 128],
                                                       IDF[:]))
            ctx.op("dve", r=[kT], w=["COLS"],
                   f=lambda e, k=k, pT=pT: e.tensor_reduce(out=COLS[:, k:k + 1],
                                                           in_=pT[:, 0:128], axis=AX.X,
                                                           op=ALU.max))
            ctx.op("pe", r=["COLS", "IDF"], w=["pb:row"],
                   f=lambda e, k=k: e.transpose(psRow[:, 1, k * 128:(k + 1) * 128],
                                                COLS[:, k:k + 1], IDF[:]))
        ctx.op("dve", r=["rows", "pb:row"], w=["rows"],
               f=lambda e: e.tensor_tensor(out=ROWS[:, 4, :], in0=ROWS[:, 3, :],
                                           in1=psRow[:, 1, :], op=ALU.mult))
        ctx.op("dve", r=["rows"], w=["rows"],
               f=lambda e: e.tensor_scalar(out=ROWS[:, 4, :], in0=ROWS[:, 4, :],
                                           scalar1=1e-5, scalar2=None, op0=ALU.max))
        ctx.op("dve", r=["rows"], w=["rows"],
               f=lambda e: e.reciprocal(out=ROWS[:, 5, :], in_=ROWS[:, 4, :]))
        ctx.op("dve", r=["rows"], w=["rows"],
               f=lambda e: e.tensor_tensor(out=ROWS[:, 5, :], in0=ROWS[:, 3, :],
                                           in1=ROWS[:, 5, :], op=ALU.mult))
        ctx.op("dve", r=["rows"], w=["rows"],
               f=lambda e: e.tensor_scalar(out=ROWS[:, 5, :], in0=ROWS[:, 5, :],
                                           scalar1=127.0, scalar2=None, op0=ALU.mult))
        ctx.op("dve", r=["rows"], w=["rows"],
               f=lambda e: e.tensor_scalar(out=ROWS[:, 4, :], in0=ROWS[:, 4, :],
                                           scalar1=1.0 / 127.0, scalar2=None, op0=ALU.mult))
        bcast(ROWS[:, 5, :], ["rows"], GBH, "GBH", PaC, "pb:C")
        ctx.op("sp", r=["rows"], w=["iah"], dma="ia",
               f=lambda e, tsl=tsl: e.dma_start(out=iah[0:1, tsl], in_=ROWS[:, 4, :]))

    def p1_y1s(it):
        X = X0 if it % 2 == 0 else X1
        kX = "X%d" % (it % 2)
        for c in range(NCHUNK):
            ctx.op("dve", r=["XB2", "GBH"], w=[kX],
                   f=lambda e, c=c, X=X: e.tensor_tensor(out=X[:, c, :], in0=XB2[:, c, :],
                                                         in1=GBH[:], op=ALU.mult))

    def p1_F(it):
        t0 = it * TT
        X = X0 if it % 2 == 0 else X1
        kX = "X%d" % (it % 2)
        th = []
        for c in range(NCHUNK):
            th.append(lambda c=c, X=X: ctx.op("act", r=[kX], w=["XI1"],
                   f=lambda e, c=c, X=X: e.activation(out=XI1[:], in_=X[:, c, :],
                                                      func=AF.Copy)))
            th.append(lambda: ctx.op("act", r=["XI1"], w=["DHP"],
                   f=lambda e: e.activation(out=DHP[:], in_=XI1[:], func=AF.Copy,
                                            bias=-8.0)))
            th.append(lambda c=c, X=X: ctx.op("dve", r=[kX, "DHP"], w=["DHP"],
                   f=lambda e, c=c, X=X: e.tensor_tensor(out=DHP[:], in0=X[:, c, :],
                                                         in1=DHP[:], op=ALU.subtract)))
            for s in (8.0, 4.0, 2.0, 1.0):
                th.append(lambda s=s: ctx.op("dve", r=["DHP"], w=["MHP"],
                       f=lambda e, s=s: e.tensor_scalar(out=MHP[:], in0=DHP[:],
                                                        scalar1=s - 0.5, scalar2=None,
                                                        op0=ALU.is_gt)))
                th.append(lambda s=s: ctx.op("dve", r=["DHP", "MHP"], w=["DHP"],
                       f=lambda e, s=s: e.scalar_tensor_tensor(out=DHP[:], in0=MHP[:],
                                                               scalar=-s, in1=DHP[:],
                                                               op0=ALU.mult, op1=ALU.add)))
            th.append(lambda c=c, X=X: ctx.op("dve", r=[kX, "DHP"], w=["XQS"],
                   f=lambda e, c=c, X=X: e.tensor_tensor(out=XQS[:], in0=X[:, c, :],
                                                         in1=DHP[:], op=ALU.subtract)))
            th.append(lambda c=c: ctx.op("pool", r=["XQS"], w=["xqh"], dma="xq",
                   f=lambda e, c=c, t0=t0: e.dma_start(
                       out=xqh[c * 128:(c + 1) * 128, t0:t0 + TT], in_=XQS[:])))
        return th

    p1_A(0)
    p1_B(0)
    pend = []
    for it in range(NT):
        p1_CD(it, pend)
        if it + 1 < NT:
            p1_A(it + 1)
        p1_E(it)
        p1_y1s(it)
        if it + 1 < NT:
            p1_B(it + 1)
        pend = p1_F(it)
    for f in pend:
        f()

    ctx.barrier()
    s1.close()

    s2 = contextlib.ExitStack()
    f1q = sbA(s2, "f1q", [128, NCHUNK, F2], BF16)
    f2q = sbA(s2, "f2q", [128, F2CHUNK, D], BF16)
    SWB = sbA(s2, "SWB", [1, 2], F32)
    THR = sbA(s2, "THR", [128, 8], F32)
    B1S = sbA(s2, "B1S", [128, F2CHUNK], F32)
    B2S = sbA(s2, "B2S", [128, NCHUNK], F32)
    ROW2 = sbA(s2, "ROW2", [1, 4, TT], F32)
    s2w = contextlib.ExitStack()
    WQ0 = sbA(s2w, "WQ0", [128, F2], F32)
    WQ1 = sbA(s2w, "WQ1", [128, F2], F32)
    WQB = sbA(s2w, "WQB", [128, F2], BF16)
    WQM = sbA(s2w, "WQM", [128, F2], BF16)

    psH0 = psA(s2, "psH0", [128, TT])
    psH1 = psA(s2, "psH1", [128, TT])
    psH2 = psA(s2, "psH2", [128, TT])
    psRowH = psA(s2, "psRowH", [1, 1, TT])
    psO = psA(s2, "psO", [128, NCHUNK, TT])
    kO2 = lambda dc: "po:%d" % (dc // 2)

    ctx.op("sp", w=["B1S"], dma="vl",
           f=lambda e: e.dma_start(out=B1S[:], in_=b1d[:]))
    ctx.op("sp", w=["B2S"], dma="vl",
           f=lambda e: e.dma_start(out=B2S[:], in_=b2d[:]))

    for wi, (wd, nch) in enumerate(((f1Td, NCHUNK), (f2Td, F2CHUNK))):
        ncols = F2 if wi == 0 else D
        nblk = ncols // TT
        for c in range(nch):
            WQ = WQ0 if c % 2 == 0 else WQ1
            kW = "WQ%d" % (c % 2)
            ctx.op("sp", w=[kW], dma="wq",
                   f=lambda e, wd=wd, c=c, WQ=WQ, ncols=ncols: e.dma_start(
                       out=WQ[:, 0:ncols], in_=wd[c * 128:(c + 1) * 128, :]))
            ctx.op("dve", r=[kW], w=["WQB"],
                   f=lambda e, WQ=WQ, ncols=ncols: e.scalar_tensor_tensor(
                       out=WQB[:, 0:ncols], in0=WQ[:, 0:ncols], scalar=-1.0,
                       in1=WQ[:, 0:ncols], op0=ALU.mult, op1=ALU.max))
            def emit_ws(e, c=c, nch=nch, nblk=nblk):
                for j in range(nblk):
                    mm = e.matmul(psRowH[:, 0, :], ONEB[:], WQB[:, j * TT:(j + 1) * TT],
                                  start=(c == 0 and j == 0),
                                  stop=(c == nch - 1 and j == nblk - 1))
                return mm
            ctx.op("pe", r=["WQB", "onesbf"], w=["ph:row"], f=emit_ws)
        ctx.op("dve", r=["ph:row"], w=["rows2"],
               f=lambda e: e.tensor_copy(out=ROW2[:, 0, :], in_=psRowH[:, 0, :]))
        ctx.op("dve", r=["rows2"], w=["rows2"],
               f=lambda e: e.tensor_reduce(out=ROW2[:, 1, 0:1], in_=ROW2[:, 0, :],
                                           axis=AX.X, op=ALU.add))
        ctx.op("dve", r=["rows2"], w=["SWB"],
               f=lambda e, wi=wi: e.tensor_scalar(out=SWB[:, wi:wi + 1],
                                                  in0=ROW2[:, 1, 0:1],
                                                  scalar1=1.0 / (D * F2), scalar2=1e-5,
                                                  op0=ALU.mult, op1=ALU.max))
    ctx.op("dve", r=["SWB"], w=["rows2"],
           f=lambda e: e.tensor_scalar(out=ROW2[:, 2, 0:2], in0=SWB[:],
                                       scalar1=0.5, scalar2=None, op0=ALU.mult))
    ctx.op("dve", r=["rows2"], w=["rows2"],
           f=lambda e: e.tensor_scalar(out=ROW2[:, 2, 2:4], in0=ROW2[:, 2, 0:2],
                                       scalar1=-1.0, scalar2=None, op0=ALU.mult))
    ctx.op("dve", r=["SWB"], w=["rows2"],
           f=lambda e: e.tensor_copy(out=ROW2[:, 2, 4:6], in_=SWB[:]))
    ctx.op("dve", w=["rows2"],
           f=lambda e: e.memset(ROW2[:, 2, 6:8], 0.0))
    ctx.op("dve", r=["rows2"], w=["rows2"],
           f=lambda e: e.tensor_copy(out=ROW2[:, 3, 0:4],
                                     in_=bass.AP(ROW2, 2 * TT, [[4 * TT, 1], [1, 2], [2, 2]])))
    ctx.op("sp", r=["rows2"], w=["thrh"], dma="vl",
           f=lambda e: e.dma_start(out=thrh_ap[:, 0:4], in_=ROW2[:, 3, 0:4]))
    ctx.op("sp", r=["rows2"], w=["thrh"], dma="vl",
           f=lambda e: e.dma_start(out=thrh_ap[:, 4:8], in_=ROW2[:, 2, 4:8]))
    ctx.op("sp", r=["thrh"], w=["THR"], dma="vl",
           f=lambda e: e.dma_start(out=THR[:], in_=bass.AP(thrh, 0, [[0, 128], [1, 8]])))
    for wi, (wd, nch, dst, kd) in enumerate(((f1Td, NCHUNK, f1q, "f1q"),
                                             (f2Td, F2CHUNK, f2q, "f2q"))):
        ncols = F2 if wi == 0 else D
        for c in range(nch):
            WQ = WQ0 if c % 2 == 0 else WQ1
            WQo = WQ1 if c % 2 == 0 else WQ0
            kW = "WQ%d" % (c % 2)
            kO = "WQ%d" % ((c + 1) % 2)
            ctx.op("sp", w=[kW], dma="wq",
                   f=lambda e, wd=wd, c=c, WQ=WQ, ncols=ncols: e.dma_start(
                       out=WQ[:, 0:ncols], in_=wd[c * 128:(c + 1) * 128, :]))
            ctx.op("dve", r=[kW, "THR"], w=["WQM"],
                   f=lambda e, WQ=WQ, wi=wi, ncols=ncols: e.tensor_scalar(
                       out=WQM[:, 0:ncols], in0=WQ[:, 0:ncols],
                       scalar1=THR[:, 2 * wi:2 * wi + 1], scalar2=None, op0=ALU.is_gt))
            ctx.op("dve", r=[kW, "THR"], w=[kW],
                   f=lambda e, WQ=WQ, wi=wi, ncols=ncols: e.tensor_scalar(
                       out=WQ[:, 0:ncols], in0=WQ[:, 0:ncols],
                       scalar1=THR[:, 2 * wi + 1:2 * wi + 2], scalar2=None, op0=ALU.is_lt))
            ctx.op("dve", r=[kW, "WQM"], w=[kd],
                   f=lambda e, WQ=WQ, dst=dst, c=c, ncols=ncols: e.tensor_tensor(
                       out=dst[:, c, :], in0=WQM[:, 0:ncols], in1=WQ[:, 0:ncols],
                       op=ALU.subtract))
    ctx.barrier()
    s2w.close()

    HHT = sbA(s2, "HHT", [128, 40, TT], FP16)
    HQT = sbA(s2, "HQT", [128, 40, TT], BF16)
    hslot = lambda it, fc: (F2CHUNK * it + fc) % 40
    XQ0 = sbA(s2, "XQ0", [128, NCHUNK, TT], BF16)
    XQ1 = sbA(s2, "XQ1", [128, NCHUNK, TT], BF16)
    XO0 = sbA(s2, "XO0", [128, NCHUNK, TT], F32)
    XO1 = sbA(s2, "XO1", [128, NCHUNK, TT], F32)
    HS0 = sbA(s2, "HS0", [128, TT], BF16)
    HS1 = sbA(s2, "HS1", [128, TT], BF16)
    PR0 = sbA(s2, "PR0", [128, TT], F32)
    PR1 = sbA(s2, "PR1", [128, TT], F32)
    HI4 = sbA(s2, "HI4", [128, 2, TT], I32)
    Y1H = sbA(s2, "Y1H", [128, 2, TT], FP16)
    DH = sbA(s2, "DH", [128, 2, TT], FP16)
    MH = sbA(s2, "MH", [128, 2, TT], FP16)
    G2BH = sbA(s2, "G2BH", [128, TT], FP16)
    MX2 = sbA(s2, "MX2", [128, TT], FP16)
    COL2 = sbA(s2, "COL2", [128, 2], F32)
    IABC = sbA(s2, "IABC", [128, TT], FP16)
    IHB = sbA(s2, "IHB", [128, TT], F32)

    def p2_loads(it):
        t0 = it * TT
        tsl = slice(t0, t0 + TT)
        XQ = XQ0 if it % 2 == 0 else XQ1
        kXQ = "XQ%d" % (it % 2)
        ctx.op("sp", r=["xqh"], w=[kXQ], dma="xl",
               f=lambda e, XQ=XQ, tsl=tsl: e.dma_start(out=XQ[:], in_=rearr(xqh)[:, :, tsl]))
        ctx.op("sp", r=["iah"], w=["rows2"], dma="ia",
               f=lambda e, tsl=tsl: e.dma_start(out=ROW2[:, 3, :], in_=iah[0:1, tsl]))
        bcast(ROW2[:, 3, :], ["rows2"], IABC, "IABC", psH0, "ph:0")

    def p2_load_xo(it):
        t0 = it * TT
        tsl = slice(t0, t0 + TT)
        XO = XO0 if it % 2 == 0 else XO1
        kXO = "XO%d" % (it % 2)
        ctx.op("sp", r=["xoh"], w=[kXO], dma="x2",
               f=lambda e, XO=XO, tsl=tsl: e.dma_start(out=XO[:], in_=rearr(xoh)[:, :, tsl]))

    def p2_m6_chunk(it, fc):
        XQ = XQ0 if it % 2 == 0 else XQ1
        kXQ = "XQ%d" % (it % 2)
        pH = (psH0, psH1, psH2)[fc % 3]
        kH = "ph:%d" % (fc % 3)
        hs_ = hslot(it, fc)
        kHc = "HHT:%d" % hs_
        def emit_m6(e, fc=fc, pH=pH, XQ=XQ):
            for c in range(NCHUNK):
                mm = e.matmul(pH[:], f1q[:, c, fc * 128:(fc + 1) * 128], XQ[:, c, :],
                              start=(c == 0), stop=(c == NCHUNK - 1))
            return mm
        ctx.op("pe", r=["f1q", kXQ], w=[kH], f=emit_m6)
        PR = PR0 if fc % 2 == 0 else PR1
        kPR = "PR%d" % (fc % 2)
        ctx.op("dve", r=[kH, "IABC"], w=[kPR],
               f=lambda e, pH=pH, PR=PR: e.tensor_tensor(out=PR[:], in0=pH[:],
                                                         in1=IABC[:], op=ALU.mult))
        ctx.op("act", r=[kPR, "THR", "B1S"], w=[kHc],
               f=lambda e, fc=fc, PR=PR, hs_=hs_: e.activation(
                   out=HHT[:, hs_, :], in_=PR[:], func=AF.Gelu,
                   bias=B1S[:, fc:fc + 1], scale=THR[:, 4:5]))
        hsq = HS0 if fc % 2 == 0 else HS1
        khs = "HS%d" % (fc % 2)
        ctx.op("act", r=[kHc], w=[khs],
               f=lambda e, hs_=hs_, hsq=hsq: e.activation(out=hsq[:], in_=HHT[:, hs_, :],
                                                          func=AF.Square))
        ctx.op("pe", r=[khs, "onesbf"], w=["ph:row"],
               f=lambda e, fc=fc, hsq=hsq: e.matmul(psRowH[:, 0, :], ONEB[:], hsq[:],
                                                    start=(fc == 0), stop=(fc == F2CHUNK - 1)))
        if fc == 0:
            ctx.op("dve", r=[kHc], w=["MX2"],
                   f=lambda e, hs_=hs_: e.tensor_copy(out=MX2[:], in_=HHT[:, hs_, :]))
            ctx.op("dve", r=[kHc, "MX2"], w=["MX2"],
                   f=lambda e, hs_=hs_: e.scalar_tensor_tensor(out=MX2[:], in0=HHT[:, hs_, :],
                                                               scalar=-1.0, in1=MX2[:],
                                                               op0=ALU.mult, op1=ALU.max))
        else:
            ctx.op("dve", r=[kHc, "MX2"], w=["MX2"],
                   f=lambda e, hs_=hs_: e.tensor_tensor(out=MX2[:], in0=MX2[:],
                                                        in1=HHT[:, hs_, :], op=ALU.max))
            ctx.op("dve", r=[kHc, "MX2"], w=["MX2"],
                   f=lambda e, hs_=hs_: e.scalar_tensor_tensor(out=MX2[:], in0=HHT[:, hs_, :],
                                                               scalar=-1.0, in1=MX2[:],
                                                               op0=ALU.mult, op1=ALU.max))

    def p2_stats(it, pend=None):
        pend = list(pend or [])

        def drain(n=1):
            for _ in range(min(n, len(pend))):
                pend.pop(0)()
        ctx.op("dve", r=["ph:row"], w=["rows2"],
               f=lambda e: e.tensor_scalar(out=ROW2[:, 0, :], in0=psRowH[:, 0, :],
                                           scalar1=1.0 / F2, scalar2=EPS_BL,
                                           op0=ALU.mult, op1=ALU.add))
        ctx.op("dve", r=["rows2"], w=["rows2"],
               f=lambda e: e.reciprocal(out=ROW2[:, 1, :], in_=ROW2[:, 0, :]))
        drain(2)
        ctx.op("act", r=["rows2"], w=["rows2"],
               f=lambda e: e.activation(out=ROW2[:, 0, :], in_=ROW2[:, 1, :], func=AF.Sqrt))
        drain(2)
        ctx.op("dve", r=["MX2"], w=["PR0"],
               f=lambda e: e.tensor_copy(out=PR0[:], in_=MX2[:]))
        drain(2)
        for k in range(TT // 128):
            pT = psH0 if k % 2 == 0 else psH1
            kT = "ph:%d" % (k % 2)
            ctx.op("pe", r=["PR0", "IDF"], w=[kT],
                   f=lambda e, k=k, pT=pT: e.transpose(pT[:, 0:128],
                                                       PR0[:, k * 128:(k + 1) * 128],
                                                       IDF[:]))
            ctx.op("dve", r=[kT], w=["COL2"],
                   f=lambda e, k=k, pT=pT: e.tensor_reduce(out=COL2[:, k:k + 1],
                                                           in_=pT[:, 0:128], axis=AX.X,
                                                           op=ALU.max))
            ctx.op("pe", r=["COL2", "IDF"], w=["ph:row"],
                   f=lambda e, k=k: e.transpose(psRowH[:, 0, k * 128:(k + 1) * 128],
                                                COL2[:, k:k + 1], IDF[:]))
            drain(3)
        ctx.op("dve", r=["rows2", "ph:row"], w=["rows2"],
               f=lambda e: e.tensor_tensor(out=ROW2[:, 1, :], in0=ROW2[:, 0, :],
                                           in1=psRowH[:, 0, :], op=ALU.mult))
        ctx.op("dve", r=["rows2"], w=["rows2"],
               f=lambda e: e.tensor_scalar(out=ROW2[:, 1, :], in0=ROW2[:, 1, :],
                                           scalar1=1e-5, scalar2=None, op0=ALU.max))
        ctx.op("dve", r=["rows2"], w=["rows2"],
               f=lambda e: e.reciprocal(out=ROW2[:, 2, :], in_=ROW2[:, 1, :]))
        ctx.op("dve", r=["rows2"], w=["rows2"],
               f=lambda e: e.tensor_tensor(out=ROW2[:, 2, :], in0=ROW2[:, 0, :],
                                           in1=ROW2[:, 2, :], op=ALU.mult))
        ctx.op("dve", r=["rows2"], w=["rows2"],
               f=lambda e: e.tensor_scalar(out=ROW2[:, 2, :], in0=ROW2[:, 2, :],
                                           scalar1=127.0, scalar2=None, op0=ALU.mult))
        ctx.op("dve", r=["rows2"], w=["rows2"],
               f=lambda e: e.tensor_scalar(out=ROW2[:, 1, :], in0=ROW2[:, 1, :],
                                           scalar1=1.0 / 127.0, scalar2=None, op0=ALU.mult))
        drain(2)
        ctx.op("dve", r=["rows2", "SWB"], w=["rows2"],
               f=lambda e: e.tensor_tensor(out=ROW2[:, 1, :], in0=ROW2[:, 1, :],
                                           in1=row01(SWB, TT, off=1, rowlen=2),
                                           op=ALU.mult))
        bcast(ROW2[:, 2, :], ["rows2"], G2BH, "G2BH", psH1, "ph:1")
        drain(len(pend))

    def p2_pair_head(it, g):
        s0 = hslot(it, 2 * g)
        sl = slice(s0, s0 + 2)
        kH0, kH1 = "HHT:%d" % s0, "HHT:%d" % (s0 + 1)
        ctx.op("dve", r=[kH0, kH1, "G2BH"], w=["Y1H"],
               f=lambda e, sl=sl: e.tensor_tensor(out=Y1H[:], in0=HHT[:, sl, :],
                                                  in1=bcast_ap(G2BH, 2, TT),
                                                  op=ALU.mult))
        ctx.op("act", r=["Y1H"], w=["HI4"],
               f=lambda e: e.activation(out=HI4[:], in_=Y1H[:], func=AF.Copy))
        ctx.op("act", r=["HI4"], w=["DH"],
               f=lambda e: e.activation(out=DH[:], in_=HI4[:], func=AF.Copy,
                                        bias=-8.0))

    def p2_pair_tail(it, g):
        s0 = hslot(it, 2 * g)
        ctx.op("dve", r=["Y1H", "DH"], w=["DH"],
               f=lambda e: e.tensor_tensor(out=DH[:], in0=Y1H[:], in1=DH[:],
                                           op=ALU.subtract))
        for s in (8.0, 4.0, 2.0, 1.0):
            ctx.op("dve", r=["DH"], w=["MH"],
                   f=lambda e, s=s: e.tensor_scalar(out=MH[:], in0=DH[:],
                                                    scalar1=s - 0.5, scalar2=None,
                                                    op0=ALU.is_gt))
            ctx.op("dve", r=["DH", "MH"], w=["DH"],
                   f=lambda e, s=s: e.scalar_tensor_tensor(out=DH[:], in0=MH[:],
                                                           scalar=-s, in1=DH[:],
                                                           op0=ALU.mult, op1=ALU.add))
        ctx.op("dve", r=["Y1H", "DH"],
               w=["HQTs:%d" % s0, "HQTs:%d" % (s0 + 1)],
               f=lambda e, s0=s0: e.tensor_tensor(out=HQT[:, s0:s0 + 2, :], in0=Y1H[:],
                                                  in1=DH[:], op=ALU.subtract))

    def p2_m7_mms(it):
        hq_keys = ["HQTs:%d" % hslot(it, fc) for fc in range(F2CHUNK)]
        th = []
        for dc in range(NCHUNK):
            def emit_m7(e, dc=dc):
                for fc in range(F2CHUNK):
                    mm = e.matmul(psO[:, dc, :], f2q[0:128, fc, dc * 128:(dc + 1) * 128],
                                  HQT[:, hslot(it, fc), :],
                                  start=(fc == 0), stop=(fc == F2CHUNK - 1))
                return mm
            th.append(lambda emit_m7=emit_m7, dc=dc: ctx.op(
                "pe", r=["f2q"] + hq_keys, w=[kO2(dc)], f=emit_m7))
        return th

    def p2_invh_bcast(it):
        bcast(ROW2[:, 1, :], ["rows2"], IHB, "IHB", psH1, "ph:1")

    def p2_m7_evac(it):
        t0 = it * TT
        tsl = slice(t0, t0 + TT)
        XO = XO0 if it % 2 == 0 else XO1
        kXO = "XO%d" % (it % 2)
        th = []
        for dc in range(NCHUNK):
            th.append(lambda dc=dc: ctx.op(
                "dve", r=[kO2(dc), "IHB"], w=["PR1"],
                f=lambda e, dc=dc: e.tensor_tensor(out=PR1[:], in0=psO[:, dc, :],
                                                   in1=IHB[:], op=ALU.mult)))
            th.append(lambda dc=dc, XO=XO: ctx.op(
                "dve", r=["PR1", "B2S", kXO], w=[kXO],
                f=lambda e, dc=dc, XO=XO: e.scalar_tensor_tensor(
                    out=XO[:, dc, :], in0=PR1[:], scalar=B2S[:, dc:dc + 1],
                    in1=XO[:, dc, :], op0=ALU.add, op1=ALU.add)))
        th.append(lambda XO=XO: ctx.op(
            "pool", r=[kXO], w=["outh"], dma="fo",
            f=lambda e, tsl=tsl, XO=XO: e.dma_start(out=rearr(outT)[:, :, tsl], in_=XO[:])))
        return th

    p2_loads(0)
    p2_load_xo(0)
    for fc in range(F2CHUNK):
        p2_m6_chunk(0, fc)
    for it in range(NT):
        mm_th = p2_m7_mms(it - 1) if it > 0 else []
        ev_th = p2_m7_evac(it - 1) if it > 0 else []
        p2_stats(it, [])
        if it + 1 < NT:
            p2_loads(it + 1)
            for fc in range(2):
                p2_m6_chunk(it + 1, fc)
        for g in range(F2CHUNK // 2):
            if it + 1 < NT and 2 * g + 3 < F2CHUNK:
                p2_m6_chunk(it + 1, 2 * g + 2)
                p2_m6_chunk(it + 1, 2 * g + 3)
            p2_pair_head(it, g)
            if it > 0:
                if g < 4:
                    mm_th[2 * g]()
                    mm_th[2 * g + 1]()
                if 4 <= g < 12:
                    ev_th[2 * (g - 4)]()
                    ev_th[2 * (g - 4) + 1]()
                if g == 12:
                    ev_th[16]()
            p2_pair_tail(it, g)
            if g == 13:
                p2_invh_bcast(it)
                if it + 1 < NT:
                    p2_load_xo(it + 1)
    for f in p2_m7_mms(NT - 1):
        f()
    for f in p2_m7_evac(NT - 1):
        f()

    ctx.barrier()
    s2.close()

    sem = {s: es.enter_context(nc.semaphore(s.replace(":", "_")))
           for s in ctx.cnt if ctx.cnt[s] > 0}
    with nc.Block() as block:
        def mk(engname):
            ops = ctx.q[engname]
            def body(e):
                for o in ops:
                    for s, v in o.waits:
                        e.wait_ge(sem[s], v)
                    if o.emit is not None:
                        instr = o.emit(e)
                        instr.then_inc(sem[o.sem], o.inc)
            return body
        block.tensor(mk("pe"))
        block.vector(mk("dve"))
        block.scalar(mk("act"))
        block.gpsimd(mk("pool"))
        block.sync(mk("sp"))
    es.close()
    return nc


def _host_inputs(inputs):
    f32c = lambda a: np.ascontiguousarray(np.asarray(a, dtype=np.float32))
    bw = f32c(inputs["L0_attn_B_w"])
    cw = f32c(inputs["L0_attn_C_w"])
    w1 = f32c(inputs["L0_attn_w1"])
    w2 = f32c(inputs["L0_attn_w2"])
    w3 = f32c(inputs["L0_attn_w3"])
    f1 = f32c(inputs["L0_ffn_w1"])
    f2 = f32c(inputs["L0_ffn_w2"])
    w2T = np.zeros((22 * 128, D), dtype=np.float32)
    w2T[:F1, :] = w2.T
    base = {
        "ll": np.ascontiguousarray(f32c(inputs["L0_attn_log_lambda"]).reshape(2, 128).T),
        "bwT": np.ascontiguousarray(bw.T),
        "cwT": np.ascontiguousarray(cw.T),
        "w1T": np.ascontiguousarray(w1.T),
        "w3T": np.ascontiguousarray(w3.T),
        "w2T": np.ascontiguousarray(w2T),
        "dsk": np.ascontiguousarray(f32c(inputs["L0_attn_D_skip"]).reshape(NCHUNK, 128).T),
        "snw": np.ascontiguousarray(f32c(inputs["L0_attn_ssm_norm_w"]).reshape(NCHUNK, 128).T),
        "fnw": np.ascontiguousarray(f32c(inputs["L0_attn_ffn_norm_w"]).reshape(NCHUNK, 128).T),
        "ident": np.eye(128, dtype=np.float32),
        "f1T": np.ascontiguousarray(f1.T),
        "b1": np.ascontiguousarray(f32c(inputs["L0_ffn_b1"]).reshape(F2CHUNK, 128).T),
        "f2T": np.ascontiguousarray(f2.T),
        "b2": np.ascontiguousarray(f32c(inputs["L0_ffn_b2"]).reshape(NCHUNK, 128).T),
    }
    return base


def kernel(**inputs):
    x = np.asarray(inputs["L0_attn_x"], dtype=np.float32)
    B = x.shape[0]
    nc = build(T=x.shape[1], TT=256)
    base = _host_inputs(inputs)
    in_maps = []
    for c in range(B):
        m = dict(base)
        m["xT"] = np.ascontiguousarray(x[c].T)
        in_maps.append(m)
    res = run_bass_kernel_spmd(nc, in_maps, core_ids=list(range(B)))
    out = np.stack([np.asarray(r["outT"]).T for r in res.results], axis=0)
    return np.ascontiguousarray(out, dtype=np.float32)


# version 31 (final)
# speedup vs baseline: 1.0267x; 1.0173x over previous
import contextlib
import numpy as np

import concourse.bass as bass
import concourse.mybir as mybir
from concourse.bass_utils import run_bass_kernel_spmd

F32 = mybir.dt.float32
BF16 = mybir.dt.bfloat16
FP16 = mybir.dt.float16
I32 = mybir.dt.int32
AF = mybir.ActivationFunctionType
ALU = mybir.AluOpType
AX = mybir.AxisListType

D = 1024
NST = 256
F1 = 2736
F2 = 4096
NCHUNK = D // 128
F2CHUNK = F2 // 128
EPS_LN = 1e-5
EPS_S = 1e-6
EPS_BL = 1e-5

FSL = [(i * 128, 128) for i in range(21)] + [(2688, 48)]

ENGS = ("pe", "dve", "act", "pool", "sp")


class Op:
    __slots__ = ("eng", "waits", "emit", "sem", "inc")

    def __init__(self, eng, waits, emit, sem, inc):
        self.eng, self.waits, self.emit, self.sem, self.inc = eng, waits, emit, sem, inc


class Ctx:

    def __init__(self, relax=True):
        self.relax = relax
        self.q = {e: [] for e in ENGS}
        self.cnt = {s: 0 for s in ENGS}
        self.waited = {e: {} for e in ENGS}
        self.writer = {}
        self.readers = {}

    def op(self, eng, r=(), w=(), f=None, dma=None):
        if dma is not None:
            sem = "D:" + dma
            inc = 16
            self.cnt.setdefault(sem, 0)
        else:
            sem = eng
            inc = 1
        need = {}
        if dma is not None and self.cnt[sem] > 0:
            need[sem] = self.cnt[sem]
        for k in r:
            st = self.writer.get(k)
            if st is not None:
                need[st[0]] = max(need.get(st[0], 0), st[1])
        for k in w:
            st = self.writer.get(k)
            if st is not None:
                need[st[0]] = max(need.get(st[0], 0), st[1])
            for s, v in self.readers.get(k, {}).items():
                need[s] = max(need.get(s, 0), v)
        waits = []
        wd = self.waited[eng]
        for s, v in need.items():
            if wd.get(s, 0) < v:
                waits.append((s, v))
                wd[s] = v
        self.cnt[sem] += inc
        stamp = (sem, self.cnt[sem])
        for k in w:
            self.writer[k] = stamp
            self.readers[k] = {}
        for k in r:
            d = self.readers.setdefault(k, {})
            d[stamp[0]] = max(d.get(stamp[0], 0), stamp[1])
        self.q[eng].append(Op(eng, waits, f, sem, inc))

    def barrier(self):
        tgt = dict(self.cnt)
        for e in ENGS:
            waits = []
            wd = self.waited[e]
            for s, v in tgt.items():
                if v > 0 and wd.get(s, 0) < v:
                    waits.append((s, v))
                    wd[s] = v
            if waits:
                self.q[e].append(Op(e, waits, None, None, 0))
        self.writer = {}
        self.readers = {}


def bcast_ap(t, chunks, tt):
    return bass.AP(t, 0, [[tt, 128], [0, chunks], [1, tt]])


def row01(t, n, off=0, rowlen=1):
    return bass.AP(t, off, [[rowlen, 1], [0, n]])


def build(T=4096, TT=256, relax=True):
    NT = T // TT
    nc = bass.Bass("TRN2", target_bir_lowering=False)

    dram = lambda name, shape, dt, kind: nc.dram_tensor(name, shape, dt, kind=kind).ap()
    xT = dram("xT", [D, T], F32, "ExternalInput")
    ll = dram("ll", [128, 2], F32, "ExternalInput")
    bwT = dram("bwT", [D, NST], F32, "ExternalInput")
    cwT = dram("cwT", [NST, D], F32, "ExternalInput")
    w1T = dram("w1T", [D, F1], F32, "ExternalInput")
    w3T = dram("w3T", [D, F1], F32, "ExternalInput")
    w2T = dram("w2T", [22 * 128, D], F32, "ExternalInput")
    dsk = dram("dsk", [128, NCHUNK], F32, "ExternalInput")
    snw = dram("snw", [128, NCHUNK], F32, "ExternalInput")
    fnw = dram("fnw", [128, NCHUNK], F32, "ExternalInput")
    identd = dram("ident", [128, 128], F32, "ExternalInput")
    f1Td = dram("f1T", [D, F2], F32, "ExternalInput")
    b1d = dram("b1", [128, F2CHUNK], F32, "ExternalInput")
    f2Td = dram("f2T", [F2, D], F32, "ExternalInput")
    b2d = dram("b2", [128, NCHUNK], F32, "ExternalInput")
    outT = dram("outT", [D, T], F32, "ExternalOutput")
    xqh = dram("xqh", [D, T], BF16, "Internal")
    xoh = dram("xoh", [D, T], F32, "Internal")
    iah = dram("iah", [1, T], F32, "Internal")
    thrh = nc.dram_tensor("thrh", [1, 8], F32, kind="Internal")
    thrh_ap = thrh.ap()

    rearr = lambda ap: ap.rearrange("(c p) t -> p c t", p=128)

    es = contextlib.ExitStack()
    ctx = Ctx(relax=relax)
    sbA = lambda st, name, shape, dt: st.enter_context(nc.sbuf_tensor(name, shape, dt))
    psA = lambda st, name, shape: st.enter_context(nc.psum_tensor(name, shape, F32))

    IDF = sbA(es, "IDF", [128, 128], F32)
    ONESR = sbA(es, "ONESR", [1, 128], F32)
    ONEB = sbA(es, "ONEB", [128, 1], BF16)

    ctx.op("sp", w=["IDF"], dma="vl",
           f=lambda e: e.dma_start(out=IDF[:], in_=identd[:]))
    ctx.op("dve", w=["ONESR"], f=lambda e: e.memset(ONESR[:], 1.0))
    ctx.op("dve", w=["onesbf"], f=lambda e: e.memset(ONEB[:], 1.0))

    def bcast(row_ap, row_keys, dst, kdst, pgrp, kpsum):
        ctx.op("pe", r=list(row_keys) + ["ONESR"], w=[kpsum],
               f=lambda e: e.matmul(pgrp[:, 0:TT], ONESR[:], row_ap,
                                    start=True, stop=True))
        ctx.op("act", r=[kpsum], w=[kdst],
               f=lambda e: e.activation(out=dst[:], in_=pgrp[:, 0:TT], func=AF.Copy))

    def round_chain(src_ap, src_keys, iq, kiq, dbuf, kd, mask_ap, kmask, emit_out):
        ctx.op("dve", r=list(src_keys), w=[kiq],
               f=lambda e: e.tensor_copy(out=iq, in_=src_ap))
        ctx.op("dve", r=list(src_keys) + [kiq], w=[kd],
               f=lambda e: e.scalar_tensor_tensor(out=dbuf, in0=iq, scalar=-1.0,
                                                  in1=src_ap, op0=ALU.mult, op1=ALU.add))
        ctx.op("dve", r=[kd], w=[kd],
               f=lambda e: e.tensor_scalar(out=dbuf, in0=dbuf, scalar1=8.0,
                                           scalar2=None, op0=ALU.add))
        for s in (8.0, 4.0, 2.0, 1.0):
            ctx.op("dve", r=[kd], w=[kmask],
                   f=lambda e, s=s: e.tensor_scalar(out=mask_ap, in0=dbuf,
                                                    scalar1=s - 0.5, scalar2=None,
                                                    op0=ALU.is_gt))
            ctx.op("dve", r=[kd, kmask], w=[kd],
                   f=lambda e, s=s: e.scalar_tensor_tensor(out=dbuf, in0=mask_ap,
                                                           scalar=-s, in1=dbuf,
                                                           op0=ALU.mult, op1=ALU.add))
        emit_out()

    s1 = contextlib.ExitStack()
    bwsb = sbA(s1, "bwsb", [128, NCHUNK, NST], BF16)
    cwsb = sbA(s1, "cwsb", [128, 2, D], BF16)
    w1sb = sbA(s1, "w1sb", [128, NCHUNK, F1], BF16)
    w3sb = sbA(s1, "w3sb", [128, NCHUNK, F1], BF16)
    w2sb = sbA(s1, "w2sb", [128, 22, D], BF16)
    X0 = sbA(s1, "X0", [128, NCHUNK, TT], F32)
    X1 = sbA(s1, "X1", [128, NCHUNK, TT], F32)
    Xbf = sbA(s1, "Xbf", [128, NCHUNK, TT], BF16)
    S1 = sbA(s1, "S1b", [128, NCHUNK, TT], BF16)
    BU = sbA(s1, "BU", [128, 2, TT], F32)
    HSB = sbA(s1, "HSB", [128, 2, TT], BF16)
    STATE = sbA(s1, "STATE", [128, 2, 1], F32)
    XB2 = sbA(s1, "XB2", [128, NCHUNK, TT], F32)
    XOUT = sbA(s1, "XOUT", [128, NCHUNK, TT], F32)
    GV = sbA(s1, "GV", [128, 22, TT], BF16)
    GS0 = sbA(s1, "GS0", [128, TT], BF16)
    GS1 = sbA(s1, "GS1", [128, TT], BF16)
    SCA = sbA(s1, "SCA", [128, TT], F32)
    SCT = sbA(s1, "SCT", [128, TT], F32)
    XI1 = sbA(s1, "XI1", [128, TT], I32)
    DHP = sbA(s1, "DHP", [128, TT], FP16)
    MHP = sbA(s1, "MHP", [128, TT], FP16)
    GBH = sbA(s1, "GBH", [128, TT], FP16)
    XQS = sbA(s1, "XQS", [128, TT], BF16)
    COLS = sbA(s1, "COLS", [128, 2], F32)
    BC = [sbA(s1, "BC%d" % i, [128, TT], F32) for i in range(3)]
    ROWS = sbA(s1, "ROWS", [1, 6, TT], F32)
    C1N = sbA(s1, "C1N", [128, 2, 1], F32)
    DP = sbA(s1, "DP", [128, NCHUNK], F32)
    SNW = sbA(s1, "SNW", [128, NCHUNK], F32)
    FNW = sbA(s1, "FNW", [128, NCHUNK], F32)
    LLS = sbA(s1, "LLS", [128, 2], F32)

    psRow = psA(s1, "psRow", [1, 2, TT])
    PaA = psA(s1, "PaA", [128, TT])
    PaB = psA(s1, "PaB", [128, TT])
    PaC = psA(s1, "PaC", [128, TT])
    psF = psA(s1, "psF", [128, NCHUNK, TT])
    kF = lambda dc: "pb:F%d" % (dc // 2)

    ctx.op("sp", w=["LLS"], dma="vl",
           f=lambda e: e.dma_start(out=LLS[:], in_=ll[:]))
    ctx.op("sp", w=["DPr"], dma="vl",
           f=lambda e: e.dma_start(out=DP[:], in_=dsk[:]))
    ctx.op("sp", w=["SNW"], dma="vl",
           f=lambda e: e.dma_start(out=SNW[:], in_=snw[:]))
    ctx.op("sp", w=["FNW"], dma="vl",
           f=lambda e: e.dma_start(out=FNW[:], in_=fnw[:]))
    ctx.op("sp", w=["X0"], dma="vl",
           f=lambda e: e.dma_start(out=X0[:], in_=rearr(bwT)))
    ctx.op("pool", w=["cw"], dma="wc",
           f=lambda e: e.dma_start(out=cwsb[:], in_=cwT.rearrange("(c p) t -> p c t", p=128)))
    ctx.op("pool", w=["w1"], dma="wc",
           f=lambda e: e.dma_start(out=w1sb[:], in_=rearr(w1T)))
    ctx.op("pool", w=["w3"], dma="wc",
           f=lambda e: e.dma_start(out=w3sb[:], in_=rearr(w3T)))
    ctx.op("pool", w=["w2"], dma="wc",
           f=lambda e: e.dma_start(out=w2sb[:], in_=w2T.rearrange("(c p) t -> p c t", p=128)))

    for c in range(NCHUNK):
        ctx.op("act", r=["X0", "SNW"], w=["bw"],
               f=lambda e, c=c: e.activation(out=bwsb[:, c, :], in_=X0[:, c, :],
                                             func=AF.Copy, scale=SNW[:, c:c + 1]))
    for n in range(2):
        def emit_c1(e, n=n):
            for c in range(NCHUNK):
                mm = e.matmul(psF[:, n, 0:1], bwsb[:, c, n * 128:(n + 1) * 128], ONEB[:],
                              start=(c == 0), stop=(c == NCHUNK - 1))
            return mm
        ctx.op("pe", r=["bw", "onesbf"], w=[kF(n)], f=emit_c1)
    for n in range(2):
        ctx.op("dve", r=[kF(n)], w=["C1N"],
               f=lambda e, n=n: e.tensor_scalar(out=C1N[:, n, :], in0=psF[:, n, 0:1],
                                                scalar1=-1.0, scalar2=None, op0=ALU.mult))
    ctx.op("dve", r=["DPr", "SNW"], w=["DP"],
           f=lambda e: e.tensor_tensor(out=DP[:], in0=DP[:], in1=SNW[:], op=ALU.mult))
    ctx.op("act", r=["LLS"], w=["LLS"],
           f=lambda e: e.activation(out=LLS[:], in_=LLS[:], func=AF.Sigmoid))
    ctx.op("dve", w=["STATE"], f=lambda e: e.memset(STATE[:], 0.0))

    def emit_sum(e, src_, row):
        for c in range(NCHUNK):
            mm = e.matmul(psRow[:, row, :], ONEB[:], src_[:, c, :],
                          start=(c == 0), stop=(c == NCHUNK - 1))
        return mm

    def p1_A(it):
        t0 = it * TT
        tsl = slice(t0, t0 + TT)
        X = X0 if it % 2 == 0 else X1
        kX = "X%d" % (it % 2)
        ctx.op("sp", w=[kX], dma="xl",
               f=lambda e, X=X, tsl=tsl: e.dma_start(out=X[:], in_=rearr(xT)[:, :, tsl]))
        ctx.op("dve", r=[kX], w=["Xbf"],
               f=lambda e, X=X: e.tensor_copy(out=Xbf[:], in_=X[:]))
        ctx.op("act", r=[kX], w=["S1"],
               f=lambda e, X=X: e.activation(out=S1[:], in_=X[:], func=AF.Square))
        ctx.op("pe", r=["Xbf", "onesbf"], w=["pb:row"], f=lambda e: emit_sum(e, Xbf, 0))
        ctx.op("pe", r=["S1", "onesbf"], w=["pb:row"],
               f=lambda e: emit_sum(e, S1, 1))
        for n in range(2):
            def emit_bu(e, n=n):
                for c in range(NCHUNK):
                    mm = e.matmul((PaA if n == 0 else PaB)[:],
                                  bwsb[:, c, n * 128:(n + 1) * 128], Xbf[:, c, :],
                                  start=(c == 0), stop=(c == NCHUNK - 1))
                return mm
            ctx.op("pe", r=["bw", "Xbf"], w=["pb:" + ("A" if n == 0 else "B")], f=emit_bu)
        ctx.op("dve", r=["pb:row"], w=["rows"],
               f=lambda e: e.tensor_scalar(out=ROWS[:, 0, :], in0=psRow[:, 0, :],
                                           scalar1=1.0 / D, scalar2=None, op0=ALU.mult))
        ctx.op("dve", r=["pb:row"], w=["rows"],
               f=lambda e: e.tensor_scalar(out=ROWS[:, 1, :], in0=psRow[:, 1, :],
                                           scalar1=1.0 / D, scalar2=None, op0=ALU.mult))
        ctx.op("dve", r=["rows"], w=["rows"],
               f=lambda e: e.tensor_tensor(out=ROWS[:, 2, :], in0=ROWS[:, 0, :],
                                           in1=ROWS[:, 0, :], op=ALU.mult))
        ctx.op("dve", r=["rows"], w=["rows"],
               f=lambda e: e.tensor_tensor(out=ROWS[:, 2, :], in0=ROWS[:, 1, :],
                                           in1=ROWS[:, 2, :], op=ALU.subtract))
        ctx.op("dve", r=["rows"], w=["rows"],
               f=lambda e: e.tensor_scalar(out=ROWS[:, 3, :], in0=ROWS[:, 2, :],
                                           scalar1=EPS_LN, scalar2=None, op0=ALU.add))
        ctx.op("dve", r=["rows"], w=["rows"],
               f=lambda e: e.tensor_scalar(out=ROWS[:, 4, :], in0=ROWS[:, 2, :],
                                           scalar1=1.0 + EPS_S, scalar2=EPS_S * EPS_LN,
                                           op0=ALU.mult, op1=ALU.add))
        ctx.op("dve", r=["rows"], w=["rows"],
               f=lambda e: e.reciprocal(out=ROWS[:, 1:3, :], in_=ROWS[:, 3:5, :]))
        ctx.op("act", r=["rows"], w=["rows"],
               f=lambda e: e.activation(out=ROWS[:, 3:5, :], in_=ROWS[:, 1:3, :], func=AF.Sqrt))
        ctx.op("dve", r=["rows"], w=["rows"],
               f=lambda e: e.tensor_tensor(out=ROWS[:, 1, :], in0=ROWS[:, 4, :],
                                           in1=ROWS[:, 0, :], op=ALU.mult))
        for bi, ri in ((0, 4), (1, 1), (2, 3)):
            bcast(ROWS[:, ri, :], ["rows"], BC[bi], "BC%d" % bi, PaC, "pb:C")
        for n in range(2):
            kP = "pb:" + ("A" if n == 0 else "B")
            ctx.op("dve", r=[kP, "BC0"], w=["BU"],
                   f=lambda e, n=n: e.tensor_tensor(out=BU[:, n, :],
                                                    in0=(PaA if n == 0 else PaB)[:],
                                                    in1=BC[0][:], op=ALU.mult))
            ctx.op("dve", r=["BC1", "C1N", "BU"], w=["BU"],
                   f=lambda e, n=n: e.scalar_tensor_tensor(out=BU[:, n, :], in0=BC[1][:],
                                                           scalar=C1N[:, n, :], in1=BU[:, n, :],
                                                           op0=ALU.mult, op1=ALU.add))
            ctx.op("dve", r=["BU", "LLS", "STATE"], w=["BU"],
                   f=lambda e, n=n: e.tensor_tensor_scan(
                       out=BU[:, n, :], data0=bass.AP(LLS, n, [[2, 128], [0, TT]]),
                       data1=BU[:, n, :], initial=STATE[:, n, :],
                       op0=ALU.mult, op1=ALU.add))
            ctx.op("dve", r=["BU"], w=["STATE"],
                   f=lambda e, n=n: e.tensor_copy(out=STATE[:, n, :], in_=BU[:, n, TT - 1:TT]))
        bcast(ROWS[:, 0, :], ["rows"], BC[1], "BC1", PaC, "pb:C")
        ctx.op("dve", r=["BU"], w=["hs"],
               f=lambda e: e.tensor_copy(out=HSB[:], in_=BU[:]))

    def p1_B(it):
        X = X0 if it % 2 == 0 else X1
        kX = "X%d" % (it % 2)
        for dc in range(NCHUNK):
            kY = "pb:" + ("A" if dc % 2 == 0 else "B")
            PY = PaA if dc % 2 == 0 else PaB
            def emit_y(e, dc=dc, PY=PY):
                for n in range(2):
                    mm = e.matmul(PY[:], cwsb[:, n, dc * 128:(dc + 1) * 128], HSB[:, n, :],
                                  start=(n == 0), stop=(n == 1))
                return mm
            ctx.op("pe", r=["cw", "hs"], w=[kY], f=emit_y)
            ctx.op("dve", r=[kX, "BC1"], w=["SCT"],
                   f=lambda e, dc=dc, X=X: e.tensor_tensor(out=SCT[:], in0=X[:, dc, :],
                                                           in1=BC[1][:], op=ALU.subtract))
            ctx.op("dve", r=["BC0", "BC2", "DP"], w=["SCA"],
                   f=lambda e, dc=dc: e.scalar_tensor_tensor(out=SCA[:], in0=BC[0][:],
                                                             scalar=DP[:, dc:dc + 1],
                                                             in1=BC[2][:],
                                                             op0=ALU.mult, op1=ALU.add))
            ctx.op("dve", r=["SCT", "SCA"], w=["SCA"],
                   f=lambda e: e.tensor_tensor(out=SCA[:], in0=SCT[:], in1=SCA[:], op=ALU.mult))
            ctx.op("dve", r=["SCA", kY], w=["XB2"],
                   f=lambda e, dc=dc, PY=PY: e.tensor_tensor(out=XB2[:, dc, :], in0=SCA[:],
                                                             in1=PY[:], op=ALU.add))
        ctx.op("act", r=["XB2"], w=["S1"],
               f=lambda e: e.activation(out=S1[:], in_=XB2[:], func=AF.Square))
        ctx.op("pe", r=["S1", "onesbf"], w=["pb:row"], f=lambda e: emit_sum(e, S1, 0))
        ctx.op("dve", r=["pb:row"], w=["rows"],
               f=lambda e: e.tensor_scalar(out=ROWS[:, 5, :], in0=psRow[:, 0, :],
                                           scalar1=1.0 / D, scalar2=EPS_S,
                                           op0=ALU.mult, op1=ALU.add))
        ctx.op("dve", r=["rows"], w=["rows"],
               f=lambda e: e.reciprocal(out=ROWS[:, 2, :], in_=ROWS[:, 5, :]))
        ctx.op("act", r=["rows"], w=["rows"],
               f=lambda e: e.activation(out=ROWS[:, 5, :], in_=ROWS[:, 2, :], func=AF.Sqrt))
        bcast(ROWS[:, 5, :], ["rows"], BC[0], "BC0", PaC, "pb:C")
        for dc in range(NCHUNK):
            ctx.op("dve", r=["XB2", "FNW", "BC0"], w=["S1"],
                   f=lambda e, dc=dc: e.scalar_tensor_tensor(out=S1[:, dc, :],
                                                             in0=XB2[:, dc, :],
                                                             scalar=FNW[:, dc:dc + 1],
                                                             in1=BC[0][:],
                                                             op0=ALU.mult, op1=ALU.mult))

    def p1_CD(it, pend=None):
        pend = list(pend or [])

        def drain(n):
            for _ in range(min(n, len(pend))):
                pend.pop(0)()
        pa_slots = [(PaA, "pb:A"), (PaB, "pb:B"), (PaC, "pb:C")]
        for fs in range(22):
            off, fsz = FSL[fs]
            pg, kg = pa_slots[(2 * fs) % 3]
            pv, kv = pa_slots[(2 * fs + 1) % 3]
            gs = GS0 if fs % 2 == 0 else GS1
            kgs = "GS%d" % (fs % 2)
            def emit_g(e, off=off, fsz=fsz, pg=pg):
                for c in range(NCHUNK):
                    mm = e.matmul(pg[0:fsz, :], w1sb[:, c, off:off + fsz], S1[:, c, :],
                                  start=(c == 0), stop=(c == NCHUNK - 1))
                return mm
            ctx.op("pe", r=["w1", "S1"], w=[kg], f=emit_g)
            def emit_v(e, off=off, fsz=fsz, pv=pv):
                for c in range(NCHUNK):
                    mm = e.matmul(pv[0:fsz, :], w3sb[:, c, off:off + fsz], S1[:, c, :],
                                  start=(c == 0), stop=(c == NCHUNK - 1))
                return mm
            ctx.op("pe", r=["w3", "S1"], w=[kv], f=emit_v)
            ctx.op("act", r=[kg], w=[kgs],
                   f=lambda e, fsz=fsz, pg=pg, gs=gs: e.activation(
                       out=gs[0:fsz, :], in_=pg[0:fsz, :], func=AF.Silu))
            ctx.op("dve", r=[kgs, kv], w=["GV"],
                   f=lambda e, fs=fs, fsz=fsz, pv=pv, gs=gs: e.tensor_tensor(
                       out=GV[0:fsz, fs, :], in0=gs[0:fsz, :], in1=pv[0:fsz, :],
                       op=ALU.mult))
            drain(3)
        for dc in range(NCHUNK):
            def emit_f(e, dc=dc):
                for fs in range(22):
                    off, fsz = FSL[fs]
                    mm = e.matmul(psF[:, dc, :], w2sb[0:fsz, fs, dc * 128:(dc + 1) * 128],
                                  GV[0:fsz, fs, :], start=(fs == 0), stop=(fs == 21))
                return mm
            ctx.op("pe", r=["w2", "GV"], w=[kF(dc)], f=emit_f)
            drain(5)
        drain(len(pend))

    def p1_E(it):
        t0 = it * TT
        tsl = slice(t0, t0 + TT)
        X = X0 if it % 2 == 0 else X1
        kX = "X%d" % (it % 2)
        ctx.op("dve", r=[kX, "XB2"], w=[kX],
               f=lambda e, X=X: e.tensor_tensor(out=X[:], in0=X[:], in1=XB2[:], op=ALU.add))
        for dc in range(NCHUNK):
            ctx.op("dve", r=[kX, kF(dc)], w=["XOUT"],
                   f=lambda e, dc=dc, X=X: e.tensor_tensor(out=XOUT[:, dc, :],
                                                           in0=X[:, dc, :],
                                                           in1=psF[:, dc, :], op=ALU.add))
        ctx.op("pool", r=["XOUT"], w=["xoh"], dma="xo",
               f=lambda e, tsl=tsl: e.dma_start(out=rearr(xoh)[:, :, tsl], in_=XOUT[:]))
        ctx.op("dve", r=["XOUT", "GV"], w=["GV"],
               f=lambda e: e.tensor_copy(out=GV[:, 0:8, :], in_=XOUT[:]))
        ctx.op("act", r=["XOUT", "GV"], w=["GV"],
               f=lambda e: e.activation(out=GV[:, 8:16, :], in_=XOUT[:], func=AF.Square))
        ctx.op("pe", r=["GV", "onesbf"], w=["pb:row"],
               f=lambda e: emit_sum(e, GV[:, 0:8, :], 0))
        ctx.op("pe", r=["GV", "onesbf"], w=["pb:row"],
               f=lambda e: emit_sum(e, GV[:, 8:16, :], 1))
        ctx.op("dve", r=["pb:row"], w=["rows"],
               f=lambda e: e.tensor_scalar(out=ROWS[:, 0, :], in0=psRow[:, 0, :],
                                           scalar1=1.0 / D, scalar2=None, op0=ALU.mult))
        ctx.op("dve", r=["pb:row"], w=["rows"],
               f=lambda e: e.tensor_scalar(out=ROWS[:, 1, :], in0=psRow[:, 1, :],
                                           scalar1=1.0 / D, scalar2=None, op0=ALU.mult))
        ctx.op("dve", r=["rows"], w=["rows"],
               f=lambda e: e.tensor_tensor(out=ROWS[:, 2, :], in0=ROWS[:, 0, :],
                                           in1=ROWS[:, 0, :], op=ALU.mult))
        ctx.op("dve", r=["rows"], w=["rows"],
               f=lambda e: e.tensor_tensor(out=ROWS[:, 2, :], in0=ROWS[:, 1, :],
                                           in1=ROWS[:, 2, :], op=ALU.subtract))
        ctx.op("dve", r=["rows"], w=["rows"],
               f=lambda e: e.tensor_scalar(out=ROWS[:, 3, :], in0=ROWS[:, 2, :],
                                           scalar1=1.0 + EPS_BL, scalar2=EPS_BL * EPS_LN,
                                           op0=ALU.mult, op1=ALU.add))
        ctx.op("dve", r=["rows"], w=["rows"],
               f=lambda e: e.reciprocal(out=ROWS[:, 4, :], in_=ROWS[:, 3, :]))
        ctx.op("act", r=["rows"], w=["rows"],
               f=lambda e: e.activation(out=ROWS[:, 3, :], in_=ROWS[:, 4, :], func=AF.Sqrt))
        bcast(ROWS[:, 0, :], ["rows"], BU[:, 0, :], "BU", PaC, "pb:C")
        ctx.op("dve", r=["XOUT", "BU"], w=["XB2"],
               f=lambda e: e.tensor_tensor(out=XB2[:], in0=XOUT[:],
                                           in1=bass.AP(BU, 0, [[2 * TT, 128],
                                                               [0, NCHUNK], [1, TT]]),
                                           op=ALU.subtract))
        ctx.op("dve", r=["XB2"], w=["SCA"],
               f=lambda e: e.tensor_tensor(out=SCA[:], in0=XB2[:, 0, :], in1=XB2[:, 1, :],
                                           op=ALU.max))
        ctx.op("dve", r=["XB2"], w=["SCT"],
               f=lambda e: e.tensor_tensor(out=SCT[:], in0=XB2[:, 0, :], in1=XB2[:, 1, :],
                                           op=ALU.min))
        for c in range(2, NCHUNK):
            ctx.op("dve", r=["XB2", "SCA"], w=["SCA"],
                   f=lambda e, c=c: e.tensor_tensor(out=SCA[:], in0=SCA[:], in1=XB2[:, c, :],
                                                    op=ALU.max))
            ctx.op("dve", r=["XB2", "SCT"], w=["SCT"],
                   f=lambda e, c=c: e.tensor_tensor(out=SCT[:], in0=SCT[:], in1=XB2[:, c, :],
                                                    op=ALU.min))
        ctx.op("dve", r=["SCT", "SCA"], w=["SCA"],
               f=lambda e: e.scalar_tensor_tensor(out=SCA[:], in0=SCT[:], scalar=-1.0,
                                                  in1=SCA[:], op0=ALU.mult, op1=ALU.max))
        for k in range(TT // 128):
            pT = PaC
            kT = "pb:C"
            ctx.op("pe", r=["SCA", "IDF"], w=[kT],
                   f=lambda e, k=k, pT=pT: e.transpose(pT[:, 0:128],
                                                       SCA[:, k * 128:(k + 1) * 128],
                                                       IDF[:]))
            ctx.op("dve", r=[kT], w=["COLS"],
                   f=lambda e, k=k, pT=pT: e.tensor_reduce(out=COLS[:, k:k + 1],
                                                           in_=pT[:, 0:128], axis=AX.X,
                                                           op=ALU.max))
            ctx.op("pe", r=["COLS", "IDF"], w=["pb:row"],
                   f=lambda e, k=k: e.transpose(psRow[:, 1, k * 128:(k + 1) * 128],
                                                COLS[:, k:k + 1], IDF[:]))
        ctx.op("dve", r=["rows", "pb:row"], w=["rows"],
               f=lambda e: e.tensor_tensor(out=ROWS[:, 4, :], in0=ROWS[:, 3, :],
                                           in1=psRow[:, 1, :], op=ALU.mult))
        ctx.op("dve", r=["rows"], w=["rows"],
               f=lambda e: e.tensor_scalar(out=ROWS[:, 4, :], in0=ROWS[:, 4, :],
                                           scalar1=1e-5, scalar2=None, op0=ALU.max))
        ctx.op("dve", r=["rows"], w=["rows"],
               f=lambda e: e.reciprocal(out=ROWS[:, 5, :], in_=ROWS[:, 4, :]))
        ctx.op("dve", r=["rows"], w=["rows"],
               f=lambda e: e.tensor_tensor(out=ROWS[:, 5, :], in0=ROWS[:, 3, :],
                                           in1=ROWS[:, 5, :], op=ALU.mult))
        ctx.op("dve", r=["rows"], w=["rows"],
               f=lambda e: e.tensor_scalar(out=ROWS[:, 5, :], in0=ROWS[:, 5, :],
                                           scalar1=127.0, scalar2=None, op0=ALU.mult))
        ctx.op("dve", r=["rows"], w=["rows"],
               f=lambda e: e.tensor_scalar(out=ROWS[:, 4, :], in0=ROWS[:, 4, :],
                                           scalar1=1.0 / 127.0, scalar2=None, op0=ALU.mult))
        bcast(ROWS[:, 5, :], ["rows"], GBH, "GBH", PaC, "pb:C")
        ctx.op("sp", r=["rows"], w=["iah"], dma="ia",
               f=lambda e, tsl=tsl: e.dma_start(out=iah[0:1, tsl], in_=ROWS[:, 4, :]))

    def p1_y1s(it):
        X = X0 if it % 2 == 0 else X1
        kX = "X%d" % (it % 2)
        for c in range(NCHUNK):
            ctx.op("dve", r=["XB2", "GBH"], w=[kX],
                   f=lambda e, c=c, X=X: e.tensor_tensor(out=X[:, c, :], in0=XB2[:, c, :],
                                                         in1=GBH[:], op=ALU.mult))

    def p1_F(it):
        t0 = it * TT
        X = X0 if it % 2 == 0 else X1
        kX = "X%d" % (it % 2)
        th = []
        for c in range(NCHUNK):
            th.append(lambda c=c, X=X: ctx.op("act", r=[kX], w=["XI1"],
                   f=lambda e, c=c, X=X: e.activation(out=XI1[:], in_=X[:, c, :],
                                                      func=AF.Copy)))
            th.append(lambda: ctx.op("act", r=["XI1"], w=["DHP"],
                   f=lambda e: e.activation(out=DHP[:], in_=XI1[:], func=AF.Copy,
                                            bias=-8.0)))
            th.append(lambda c=c, X=X: ctx.op("dve", r=[kX, "DHP"], w=["DHP"],
                   f=lambda e, c=c, X=X: e.tensor_tensor(out=DHP[:], in0=X[:, c, :],
                                                         in1=DHP[:], op=ALU.subtract)))
            for s in (8.0, 4.0, 2.0, 1.0):
                th.append(lambda s=s: ctx.op("dve", r=["DHP"], w=["MHP"],
                       f=lambda e, s=s: e.tensor_scalar(out=MHP[:], in0=DHP[:],
                                                        scalar1=s - 0.5, scalar2=None,
                                                        op0=ALU.is_gt)))
                th.append(lambda s=s: ctx.op("dve", r=["DHP", "MHP"], w=["DHP"],
                       f=lambda e, s=s: e.scalar_tensor_tensor(out=DHP[:], in0=MHP[:],
                                                               scalar=-s, in1=DHP[:],
                                                               op0=ALU.mult, op1=ALU.add)))
            th.append(lambda c=c, X=X: ctx.op("dve", r=[kX, "DHP"], w=["XQS"],
                   f=lambda e, c=c, X=X: e.tensor_tensor(out=XQS[:], in0=X[:, c, :],
                                                         in1=DHP[:], op=ALU.subtract)))
            th.append(lambda c=c: ctx.op("pool", r=["XQS"], w=["xqh"], dma="xq",
                   f=lambda e, c=c, t0=t0: e.dma_start(
                       out=xqh[c * 128:(c + 1) * 128, t0:t0 + TT], in_=XQS[:])))
        return th

    p1_A(0)
    p1_B(0)
    pend = []
    for it in range(NT):
        p1_CD(it, pend)
        if it + 1 < NT:
            p1_A(it + 1)
        p1_E(it)
        p1_y1s(it)
        if it + 1 < NT:
            p1_B(it + 1)
        pend = p1_F(it)
    for f in pend:
        f()

    ctx.barrier()
    s1.close()

    s2 = contextlib.ExitStack()
    f1q = sbA(s2, "f1q", [128, NCHUNK, F2], BF16)
    f2q = sbA(s2, "f2q", [128, F2CHUNK, D], BF16)
    SWB = sbA(s2, "SWB", [1, 2], F32)
    THR = sbA(s2, "THR", [128, 8], F32)
    B1S = sbA(s2, "B1S", [128, F2CHUNK], F32)
    B2S = sbA(s2, "B2S", [128, NCHUNK], F32)
    ROW2 = sbA(s2, "ROW2", [1, 4, TT], F32)
    s2w = contextlib.ExitStack()
    WQ0 = sbA(s2w, "WQ0", [128, F2], F32)
    WQ1 = sbA(s2w, "WQ1", [128, F2], F32)
    WQB = sbA(s2w, "WQB", [128, F2], BF16)
    WQM = sbA(s2w, "WQM", [128, F2], BF16)

    psH0 = psA(s2, "psH0", [128, TT])
    psH1 = psA(s2, "psH1", [128, TT])
    psH2 = psA(s2, "psH2", [128, TT])
    psRowH = psA(s2, "psRowH", [1, 1, TT])
    psO = psA(s2, "psO", [128, NCHUNK, TT])
    kO2 = lambda dc: "po:%d" % (dc // 2)

    ctx.op("sp", w=["B1S"], dma="vl",
           f=lambda e: e.dma_start(out=B1S[:], in_=b1d[:]))
    ctx.op("sp", w=["B2S"], dma="vl",
           f=lambda e: e.dma_start(out=B2S[:], in_=b2d[:]))

    for wi, (wd, nch) in enumerate(((f1Td, NCHUNK), (f2Td, F2CHUNK))):
        ncols = F2 if wi == 0 else D
        nblk = ncols // TT
        for c in range(nch):
            WQ = WQ0 if c % 2 == 0 else WQ1
            kW = "WQ%d" % (c % 2)
            ctx.op("sp", w=[kW], dma="wq",
                   f=lambda e, wd=wd, c=c, WQ=WQ, ncols=ncols: e.dma_start(
                       out=WQ[:, 0:ncols], in_=wd[c * 128:(c + 1) * 128, :]))
            ctx.op("dve", r=[kW], w=["WQB"],
                   f=lambda e, WQ=WQ, ncols=ncols: e.scalar_tensor_tensor(
                       out=WQB[:, 0:ncols], in0=WQ[:, 0:ncols], scalar=-1.0,
                       in1=WQ[:, 0:ncols], op0=ALU.mult, op1=ALU.max))
            def emit_ws(e, c=c, nch=nch, nblk=nblk):
                for j in range(nblk):
                    mm = e.matmul(psRowH[:, 0, :], ONEB[:], WQB[:, j * TT:(j + 1) * TT],
                                  start=(c == 0 and j == 0),
                                  stop=(c == nch - 1 and j == nblk - 1))
                return mm
            ctx.op("pe", r=["WQB", "onesbf"], w=["ph:row"], f=emit_ws)
        ctx.op("dve", r=["ph:row"], w=["rows2"],
               f=lambda e: e.tensor_copy(out=ROW2[:, 0, :], in_=psRowH[:, 0, :]))
        ctx.op("dve", r=["rows2"], w=["rows2"],
               f=lambda e: e.tensor_reduce(out=ROW2[:, 1, 0:1], in_=ROW2[:, 0, :],
                                           axis=AX.X, op=ALU.add))
        ctx.op("dve", r=["rows2"], w=["SWB"],
               f=lambda e, wi=wi: e.tensor_scalar(out=SWB[:, wi:wi + 1],
                                                  in0=ROW2[:, 1, 0:1],
                                                  scalar1=1.0 / (D * F2), scalar2=1e-5,
                                                  op0=ALU.mult, op1=ALU.max))
    ctx.op("dve", r=["SWB"], w=["rows2"],
           f=lambda e: e.tensor_scalar(out=ROW2[:, 2, 0:2], in0=SWB[:],
                                       scalar1=0.5, scalar2=None, op0=ALU.mult))
    ctx.op("dve", r=["rows2"], w=["rows2"],
           f=lambda e: e.tensor_scalar(out=ROW2[:, 2, 2:4], in0=ROW2[:, 2, 0:2],
                                       scalar1=-1.0, scalar2=None, op0=ALU.mult))
    ctx.op("dve", r=["SWB"], w=["rows2"],
           f=lambda e: e.tensor_copy(out=ROW2[:, 2, 4:6], in_=SWB[:]))
    ctx.op("dve", w=["rows2"],
           f=lambda e: e.memset(ROW2[:, 2, 6:8], 0.0))
    ctx.op("dve", r=["rows2"], w=["rows2"],
           f=lambda e: e.tensor_copy(out=ROW2[:, 3, 0:4],
                                     in_=bass.AP(ROW2, 2 * TT, [[4 * TT, 1], [1, 2], [2, 2]])))
    ctx.op("sp", r=["rows2"], w=["thrh"], dma="vl",
           f=lambda e: e.dma_start(out=thrh_ap[:, 0:4], in_=ROW2[:, 3, 0:4]))
    ctx.op("sp", r=["rows2"], w=["thrh"], dma="vl",
           f=lambda e: e.dma_start(out=thrh_ap[:, 4:8], in_=ROW2[:, 2, 4:8]))
    ctx.op("sp", r=["thrh"], w=["THR"], dma="vl",
           f=lambda e: e.dma_start(out=THR[:], in_=bass.AP(thrh, 0, [[0, 128], [1, 8]])))
    for wi, (wd, nch, dst, kd) in enumerate(((f1Td, NCHUNK, f1q, "f1q"),
                                             (f2Td, F2CHUNK, f2q, "f2q"))):
        ncols = F2 if wi == 0 else D
        for c in range(nch):
            WQ = WQ0 if c % 2 == 0 else WQ1
            WQo = WQ1 if c % 2 == 0 else WQ0
            kW = "WQ%d" % (c % 2)
            kO = "WQ%d" % ((c + 1) % 2)
            ctx.op("sp", w=[kW], dma="wq",
                   f=lambda e, wd=wd, c=c, WQ=WQ, ncols=ncols: e.dma_start(
                       out=WQ[:, 0:ncols], in_=wd[c * 128:(c + 1) * 128, :]))
            ctx.op("dve", r=[kW, "THR"], w=["WQM"],
                   f=lambda e, WQ=WQ, wi=wi, ncols=ncols: e.tensor_scalar(
                       out=WQM[:, 0:ncols], in0=WQ[:, 0:ncols],
                       scalar1=THR[:, 2 * wi:2 * wi + 1], scalar2=None, op0=ALU.is_gt))
            ctx.op("dve", r=[kW, "THR"], w=[kW],
                   f=lambda e, WQ=WQ, wi=wi, ncols=ncols: e.tensor_scalar(
                       out=WQ[:, 0:ncols], in0=WQ[:, 0:ncols],
                       scalar1=THR[:, 2 * wi + 1:2 * wi + 2], scalar2=None, op0=ALU.is_lt))
            ctx.op("dve", r=[kW, "WQM"], w=[kd],
                   f=lambda e, WQ=WQ, dst=dst, c=c, ncols=ncols: e.tensor_tensor(
                       out=dst[:, c, :], in0=WQM[:, 0:ncols], in1=WQ[:, 0:ncols],
                       op=ALU.subtract))
    ctx.barrier()
    s2w.close()

    HHT = sbA(s2, "HHT", [128, 40, TT], FP16)
    HQT = sbA(s2, "HQT", [128, 40, TT], BF16)
    hslot = lambda it, fc: (F2CHUNK * it + fc) % 40
    XQ0 = sbA(s2, "XQ0", [128, NCHUNK, TT], BF16)
    XQ1 = sbA(s2, "XQ1", [128, NCHUNK, TT], BF16)
    XO0 = sbA(s2, "XO0", [128, NCHUNK, TT], F32)
    XO1 = sbA(s2, "XO1", [128, NCHUNK, TT], F32)
    HS0 = sbA(s2, "HS0", [128, TT], BF16)
    HS1 = sbA(s2, "HS1", [128, TT], BF16)
    PR0 = sbA(s2, "PR0", [128, TT], F32)
    PR1 = sbA(s2, "PR1", [128, TT], F32)
    HI4 = sbA(s2, "HI4", [128, 2, TT], I32)
    Y1H = sbA(s2, "Y1H", [128, 2, TT], FP16)
    DH = sbA(s2, "DH", [128, 2, TT], FP16)
    MH = sbA(s2, "MH", [128, 2, TT], FP16)
    G2BH = sbA(s2, "G2BH", [128, TT], FP16)
    MX2 = sbA(s2, "MX2", [128, TT], FP16)
    COL2 = sbA(s2, "COL2", [128, 2], F32)
    IABC = sbA(s2, "IABC", [128, TT], FP16)
    IHB = sbA(s2, "IHB", [128, TT], F32)

    def p2_loads(it):
        t0 = it * TT
        tsl = slice(t0, t0 + TT)
        XQ = XQ0 if it % 2 == 0 else XQ1
        kXQ = "XQ%d" % (it % 2)
        ctx.op("sp", r=["xqh"], w=[kXQ], dma="xl",
               f=lambda e, XQ=XQ, tsl=tsl: e.dma_start(out=XQ[:], in_=rearr(xqh)[:, :, tsl]))
        ctx.op("sp", r=["iah"], w=["rows2"], dma="ia",
               f=lambda e, tsl=tsl: e.dma_start(out=ROW2[:, 3, :], in_=iah[0:1, tsl]))
        bcast(ROW2[:, 3, :], ["rows2"], IABC, "IABC", psH0, "ph:0")

    def p2_load_xo(it):
        t0 = it * TT
        tsl = slice(t0, t0 + TT)
        XO = XO0 if it % 2 == 0 else XO1
        kXO = "XO%d" % (it % 2)
        ctx.op("sp", r=["xoh"], w=[kXO], dma="x2",
               f=lambda e, XO=XO, tsl=tsl: e.dma_start(out=XO[:], in_=rearr(xoh)[:, :, tsl]))

    def p2_m6_chunk(it, fc):
        XQ = XQ0 if it % 2 == 0 else XQ1
        kXQ = "XQ%d" % (it % 2)
        pH = (psH0, psH1, psH2)[fc % 3]
        kH = "ph:%d" % (fc % 3)
        hs_ = hslot(it, fc)
        kHc = "HHT:%d" % hs_
        def emit_m6(e, fc=fc, pH=pH, XQ=XQ):
            for c in range(NCHUNK):
                mm = e.matmul(pH[:], f1q[:, c, fc * 128:(fc + 1) * 128], XQ[:, c, :],
                              start=(c == 0), stop=(c == NCHUNK - 1))
            return mm
        ctx.op("pe", r=["f1q", kXQ], w=[kH], f=emit_m6)
        PR = PR0 if fc % 2 == 0 else PR1
        kPR = "PR%d" % (fc % 2)
        ctx.op("dve", r=[kH, "IABC"], w=[kPR],
               f=lambda e, pH=pH, PR=PR: e.tensor_tensor(out=PR[:], in0=pH[:],
                                                         in1=IABC[:], op=ALU.mult))
        ctx.op("act", r=[kPR, "THR", "B1S"], w=[kHc],
               f=lambda e, fc=fc, PR=PR, hs_=hs_: e.activation(
                   out=HHT[:, hs_, :], in_=PR[:], func=AF.Gelu,
                   bias=B1S[:, fc:fc + 1], scale=THR[:, 4:5]))
        hsq = HS0 if fc % 2 == 0 else HS1
        khs = "HS%d" % (fc % 2)
        ctx.op("act", r=[kHc], w=[khs],
               f=lambda e, hs_=hs_, hsq=hsq: e.activation(out=hsq[:], in_=HHT[:, hs_, :],
                                                          func=AF.Square))
        ctx.op("pe", r=[khs, "onesbf"], w=["ph:row"],
               f=lambda e, fc=fc, hsq=hsq: e.matmul(psRowH[:, 0, :], ONEB[:], hsq[:],
                                                    start=(fc == 0), stop=(fc == F2CHUNK - 1)))
        if fc == 0:
            ctx.op("dve", r=[kHc], w=["MX2"],
                   f=lambda e, hs_=hs_: e.tensor_copy(out=MX2[:], in_=HHT[:, hs_, :]))
            ctx.op("dve", r=[kHc, "MX2"], w=["MX2"],
                   f=lambda e, hs_=hs_: e.scalar_tensor_tensor(out=MX2[:], in0=HHT[:, hs_, :],
                                                               scalar=-1.0, in1=MX2[:],
                                                               op0=ALU.mult, op1=ALU.max))
        else:
            ctx.op("dve", r=[kHc, "MX2"], w=["MX2"],
                   f=lambda e, hs_=hs_: e.tensor_tensor(out=MX2[:], in0=MX2[:],
                                                        in1=HHT[:, hs_, :], op=ALU.max))
            ctx.op("dve", r=[kHc, "MX2"], w=["MX2"],
                   f=lambda e, hs_=hs_: e.scalar_tensor_tensor(out=MX2[:], in0=HHT[:, hs_, :],
                                                               scalar=-1.0, in1=MX2[:],
                                                               op0=ALU.mult, op1=ALU.max))

    def p2_stats(it, pend=None):
        pend = list(pend or [])

        def drain(n=1):
            for _ in range(min(n, len(pend))):
                pend.pop(0)()
        ctx.op("dve", r=["ph:row"], w=["rows2"],
               f=lambda e: e.tensor_scalar(out=ROW2[:, 0, :], in0=psRowH[:, 0, :],
                                           scalar1=1.0 / F2, scalar2=EPS_BL,
                                           op0=ALU.mult, op1=ALU.add))
        ctx.op("dve", r=["rows2"], w=["rows2"],
               f=lambda e: e.reciprocal(out=ROW2[:, 1, :], in_=ROW2[:, 0, :]))
        drain(2)
        ctx.op("act", r=["rows2"], w=["rows2"],
               f=lambda e: e.activation(out=ROW2[:, 0, :], in_=ROW2[:, 1, :], func=AF.Sqrt))
        drain(2)
        ctx.op("dve", r=["MX2"], w=["PR0"],
               f=lambda e: e.tensor_copy(out=PR0[:], in_=MX2[:]))
        drain(2)
        for k in range(TT // 128):
            pT = psH0 if k % 2 == 0 else psH1
            kT = "ph:%d" % (k % 2)
            ctx.op("pe", r=["PR0", "IDF"], w=[kT],
                   f=lambda e, k=k, pT=pT: e.transpose(pT[:, 0:128],
                                                       PR0[:, k * 128:(k + 1) * 128],
                                                       IDF[:]))
            ctx.op("dve", r=[kT], w=["COL2"],
                   f=lambda e, k=k, pT=pT: e.tensor_reduce(out=COL2[:, k:k + 1],
                                                           in_=pT[:, 0:128], axis=AX.X,
                                                           op=ALU.max))
            ctx.op("pe", r=["COL2", "IDF"], w=["ph:row"],
                   f=lambda e, k=k: e.transpose(psRowH[:, 0, k * 128:(k + 1) * 128],
                                                COL2[:, k:k + 1], IDF[:]))
            drain(3)
        ctx.op("dve", r=["rows2", "ph:row"], w=["rows2"],
               f=lambda e: e.tensor_tensor(out=ROW2[:, 1, :], in0=ROW2[:, 0, :],
                                           in1=psRowH[:, 0, :], op=ALU.mult))
        ctx.op("dve", r=["rows2"], w=["rows2"],
               f=lambda e: e.tensor_scalar(out=ROW2[:, 1, :], in0=ROW2[:, 1, :],
                                           scalar1=1e-5, scalar2=None, op0=ALU.max))
        ctx.op("dve", r=["rows2"], w=["rows2"],
               f=lambda e: e.reciprocal(out=ROW2[:, 2, :], in_=ROW2[:, 1, :]))
        ctx.op("dve", r=["rows2"], w=["rows2"],
               f=lambda e: e.tensor_tensor(out=ROW2[:, 2, :], in0=ROW2[:, 0, :],
                                           in1=ROW2[:, 2, :], op=ALU.mult))
        ctx.op("dve", r=["rows2"], w=["rows2"],
               f=lambda e: e.tensor_scalar(out=ROW2[:, 2, :], in0=ROW2[:, 2, :],
                                           scalar1=127.0, scalar2=None, op0=ALU.mult))
        ctx.op("dve", r=["rows2"], w=["rows2"],
               f=lambda e: e.tensor_scalar(out=ROW2[:, 1, :], in0=ROW2[:, 1, :],
                                           scalar1=1.0 / 127.0, scalar2=None, op0=ALU.mult))
        drain(2)
        ctx.op("dve", r=["rows2", "SWB"], w=["rows2"],
               f=lambda e: e.tensor_tensor(out=ROW2[:, 1, :], in0=ROW2[:, 1, :],
                                           in1=row01(SWB, TT, off=1, rowlen=2),
                                           op=ALU.mult))
        bcast(ROW2[:, 2, :], ["rows2"], G2BH, "G2BH", psH1, "ph:1")
        drain(len(pend))

    def p2_pair_head(it, g):
        s0 = hslot(it, 2 * g)
        sl = slice(s0, s0 + 2)
        kH0, kH1 = "HHT:%d" % s0, "HHT:%d" % (s0 + 1)
        ctx.op("dve", r=[kH0, kH1, "G2BH"], w=["Y1H"],
               f=lambda e, sl=sl: e.tensor_tensor(out=Y1H[:], in0=HHT[:, sl, :],
                                                  in1=bcast_ap(G2BH, 2, TT),
                                                  op=ALU.mult))
        ctx.op("act", r=["Y1H"], w=["HI4"],
               f=lambda e: e.activation(out=HI4[:], in_=Y1H[:], func=AF.Copy))
        ctx.op("act", r=["HI4"], w=["DH"],
               f=lambda e: e.activation(out=DH[:], in_=HI4[:], func=AF.Copy,
                                        bias=-8.0))

    def p2_pair_tail(it, g):
        s0 = hslot(it, 2 * g)
        ctx.op("dve", r=["Y1H", "DH"], w=["DH"],
               f=lambda e: e.tensor_tensor(out=DH[:], in0=Y1H[:], in1=DH[:],
                                           op=ALU.subtract))
        for s in (8.0, 4.0, 2.0, 1.0):
            ctx.op("dve", r=["DH"], w=["MH"],
                   f=lambda e, s=s: e.tensor_scalar(out=MH[:], in0=DH[:],
                                                    scalar1=s - 0.5, scalar2=None,
                                                    op0=ALU.is_gt))
            ctx.op("dve", r=["DH", "MH"], w=["DH"],
                   f=lambda e, s=s: e.scalar_tensor_tensor(out=DH[:], in0=MH[:],
                                                           scalar=-s, in1=DH[:],
                                                           op0=ALU.mult, op1=ALU.add))
        ctx.op("dve", r=["Y1H", "DH"],
               w=["HQTs:%d" % s0, "HQTs:%d" % (s0 + 1)],
               f=lambda e, s0=s0: e.tensor_tensor(out=HQT[:, s0:s0 + 2, :], in0=Y1H[:],
                                                  in1=DH[:], op=ALU.subtract))

    def p2_m7_mms(it):
        hq_keys = ["HQTs:%d" % hslot(it, fc) for fc in range(F2CHUNK)]
        th = []
        for dc in range(NCHUNK):
            def emit_m7(e, dc=dc):
                for fc in range(F2CHUNK):
                    mm = e.matmul(psO[:, dc, :], f2q[0:128, fc, dc * 128:(dc + 1) * 128],
                                  HQT[:, hslot(it, fc), :],
                                  start=(fc == 0), stop=(fc == F2CHUNK - 1))
                return mm
            th.append(lambda emit_m7=emit_m7, dc=dc: ctx.op(
                "pe", r=["f2q"] + hq_keys, w=[kO2(dc)], f=emit_m7))
        return th

    def p2_invh_bcast(it):
        bcast(ROW2[:, 1, :], ["rows2"], IHB, "IHB", psH1, "ph:1")

    def p2_m7_evac(it):
        t0 = it * TT
        tsl = slice(t0, t0 + TT)
        XO = XO0 if it % 2 == 0 else XO1
        kXO = "XO%d" % (it % 2)
        th = []
        for dc in range(NCHUNK):
            th.append(lambda dc=dc: ctx.op(
                "dve", r=[kO2(dc), "IHB"], w=["PR1"],
                f=lambda e, dc=dc: e.tensor_tensor(out=PR1[:], in0=psO[:, dc, :],
                                                   in1=IHB[:], op=ALU.mult)))
            th.append(lambda dc=dc, XO=XO: ctx.op(
                "dve", r=["PR1", "B2S", kXO], w=[kXO],
                f=lambda e, dc=dc, XO=XO: e.scalar_tensor_tensor(
                    out=XO[:, dc, :], in0=PR1[:], scalar=B2S[:, dc:dc + 1],
                    in1=XO[:, dc, :], op0=ALU.add, op1=ALU.add)))
        th.append(lambda XO=XO: ctx.op(
            "pool", r=[kXO], w=["outh"], dma="fo",
            f=lambda e, tsl=tsl, XO=XO: e.dma_start(out=rearr(outT)[:, :, tsl], in_=XO[:])))
        return th

    p2_loads(0)
    p2_load_xo(0)
    for fc in range(F2CHUNK):
        p2_m6_chunk(0, fc)
    for it in range(NT):
        mm_th = p2_m7_mms(it - 1) if it > 0 else []
        ev_th = p2_m7_evac(it - 1) if it > 0 else []
        p2_stats(it, [])
        if it + 1 < NT:
            p2_loads(it + 1)
        for g in range(F2CHUNK // 2):
            if it + 1 < NT:
                p2_m6_chunk(it + 1, 2 * g)
                p2_m6_chunk(it + 1, 2 * g + 1)
            p2_pair_head(it, g)
            if it > 0:
                if g < 4:
                    mm_th[2 * g]()
                    mm_th[2 * g + 1]()
                if 4 <= g < 12:
                    ev_th[2 * (g - 4)]()
                    ev_th[2 * (g - 4) + 1]()
                if g == 12:
                    ev_th[16]()
            p2_pair_tail(it, g)
            if g == 13:
                p2_invh_bcast(it)
                if it + 1 < NT:
                    p2_load_xo(it + 1)
    for f in p2_m7_mms(NT - 1):
        f()
    for f in p2_m7_evac(NT - 1):
        f()

    ctx.barrier()
    s2.close()

    sem = {s: es.enter_context(nc.semaphore(s.replace(":", "_")))
           for s in ctx.cnt if ctx.cnt[s] > 0}
    with nc.Block() as block:
        def mk(engname):
            ops = ctx.q[engname]
            def body(e):
                for o in ops:
                    for s, v in o.waits:
                        e.wait_ge(sem[s], v)
                    if o.emit is not None:
                        instr = o.emit(e)
                        instr.then_inc(sem[o.sem], o.inc)
            return body
        block.tensor(mk("pe"))
        block.vector(mk("dve"))
        block.scalar(mk("act"))
        block.gpsimd(mk("pool"))
        block.sync(mk("sp"))
    es.close()
    return nc


def _host_inputs(inputs):
    f32c = lambda a: np.ascontiguousarray(np.asarray(a, dtype=np.float32))
    bw = f32c(inputs["L0_attn_B_w"])
    cw = f32c(inputs["L0_attn_C_w"])
    w1 = f32c(inputs["L0_attn_w1"])
    w2 = f32c(inputs["L0_attn_w2"])
    w3 = f32c(inputs["L0_attn_w3"])
    f1 = f32c(inputs["L0_ffn_w1"])
    f2 = f32c(inputs["L0_ffn_w2"])
    w2T = np.zeros((22 * 128, D), dtype=np.float32)
    w2T[:F1, :] = w2.T
    base = {
        "ll": np.ascontiguousarray(f32c(inputs["L0_attn_log_lambda"]).reshape(2, 128).T),
        "bwT": np.ascontiguousarray(bw.T),
        "cwT": np.ascontiguousarray(cw.T),
        "w1T": np.ascontiguousarray(w1.T),
        "w3T": np.ascontiguousarray(w3.T),
        "w2T": np.ascontiguousarray(w2T),
        "dsk": np.ascontiguousarray(f32c(inputs["L0_attn_D_skip"]).reshape(NCHUNK, 128).T),
        "snw": np.ascontiguousarray(f32c(inputs["L0_attn_ssm_norm_w"]).reshape(NCHUNK, 128).T),
        "fnw": np.ascontiguousarray(f32c(inputs["L0_attn_ffn_norm_w"]).reshape(NCHUNK, 128).T),
        "ident": np.eye(128, dtype=np.float32),
        "f1T": np.ascontiguousarray(f1.T),
        "b1": np.ascontiguousarray(f32c(inputs["L0_ffn_b1"]).reshape(F2CHUNK, 128).T),
        "f2T": np.ascontiguousarray(f2.T),
        "b2": np.ascontiguousarray(f32c(inputs["L0_ffn_b2"]).reshape(NCHUNK, 128).T),
    }
    return base


def kernel(**inputs):
    x = np.asarray(inputs["L0_attn_x"], dtype=np.float32)
    B = x.shape[0]
    nc = build(T=x.shape[1], TT=256)
    base = _host_inputs(inputs)
    in_maps = []
    for c in range(B):
        m = dict(base)
        m["xT"] = np.ascontiguousarray(x[c].T)
        in_maps.append(m)
    res = run_bass_kernel_spmd(nc, in_maps, core_ids=list(range(B)))
    out = np.stack([np.asarray(r["outT"]).T for r in res.results], axis=0)
    return np.ascontiguousarray(out, dtype=np.float32)
